# version 01 (final)
# baseline (speedup 1.0000x reference)
import numpy as np
import ml_dtypes
from contextlib import ExitStack
import concourse.bass as bass
import concourse.mybir as mybir
from concourse.bass_utils import run_bass_kernel_spmd

F32 = mybir.dt.float32
BF16 = mybir.dt.bfloat16
I8 = mybir.dt.int8
AF = mybir.ActivationFunctionType
ALU = mybir.AluOpType
AX = mybir.AxisListType
BF = ml_dtypes.bfloat16

NCORES = 8
T = 1024
S = 4096
DM = 4096
NFT = DM // 128
LN_EPS = 1e-5


def core_positions(c):
    j = c % 4
    return np.arange(1024 * j, 1024 * (j + 1))

def rope_tables_np():
    HD = 128
    inv_freq = 1.0 / (10000.0 ** (np.arange(0, HD, 2, dtype=np.float64) / HD))
    freqs = np.arange(S, dtype=np.float64)[:, None] * inv_freq
    emb = np.concatenate([freqs, freqs], axis=-1)
    return np.cos(emb).astype(np.float32), np.sin(emb).astype(np.float32)

def rot_matrix_T_np():
    R = np.zeros((128, 128), dtype=np.float32)
    for d in range(64):
        R[d, 64 + d] = -1.0
    for d in range(64, 128):
        R[d, d - 64] = 1.0
    return np.ascontiguousarray(R.T)

def sel32_np():
    E = np.zeros((32, 32 * 128), dtype=np.float32)
    for g in range(32):
        E[g, g * 128:(g + 1) * 128] = 1.0
    return E


class BC:
    ENGS = ("pe", "act", "dve", "pool", "qs", "qg")

    def __init__(self, nc, E):
        self.nc = nc
        self.E = E
        self.blk = E(nc.Block())
        self.sem = {n: E(nc.semaphore("S" + n)) for n in self.ENGS}
        self.cnt = {n: 0 for n in self.ENGS}

    def xsem(self, name):
        if name not in self.sem:
            self.sem[name] = self.E(self.nc.semaphore("S" + name))
            self.cnt[name] = 0
        return name

    def tick(self, instr, eng, n=1):
        self.cnt[eng] += n
        instr.then_inc(self.sem[eng], n)
        return self.cnt[eng]

    enabled = True

    def on(self, engname, fn):
        if not self.enabled:
            return
        getattr(self.blk, {"pe": "tensor", "act": "scalar", "dve": "vector",
                           "pool": "gpsimd", "qs": "sync", "qg": "gpsimd"}[engname])(fn)

    def sedge(self, e, instr):
        self.tick(instr, "dve", 1)
        e.wait_ge(self.sem["dve"], self.cnt["dve"])

    def wait(self, e, eng, val=None):
        v = self.cnt[eng] if val is None else val
        if v > 0:
            e.wait_ge(self.sem[eng], v)

    def barrier_into(self, engname, *producers):
        snap = {p: self.cnt[p] for p in producers}
        def f(e):
            for p, v in snap.items():
                if v > 0:
                    e.wait_ge(self.sem[p], v)
        self.on(engname, f)



QKV_W = 6144

def build_A():
    nc = bass.Bass("TRN2", target_bir_lowering=False)
    xT = nc.dram_tensor("xT", [DM, T], F32, kind="ExternalInput").ap()
    qw = nc.dram_tensor("qw_cat", [DM, QKV_W], I8, kind="ExternalInput").ap()
    sc = nc.dram_tensor("sc_cat", [32, QKV_W], F32, kind="ExternalInput").ap()
    cosT = nc.dram_tensor("cosT", [128, T], F32, kind="ExternalInput").ap()
    sinT = nc.dram_tensor("sinT", [128, T], F32, kind="ExternalInput").ap()
    o_q = nc.dram_tensor("qro", [DM, T], BF16, kind="ExternalOutput").ap()
    o_kv = nc.dram_tensor("kv", [2048, T], BF16, kind="ExternalOutput").ap()

    ones_c = nc.inline_tensor(np.ones((128, 1), dtype=BF), "c_ones_c")
    ones_r = nc.inline_tensor(np.ones((1, 128), dtype=np.float32), "c_ones_r")
    rotT = nc.inline_tensor(rot_matrix_T_np().astype(BF), "c_rotT")
    e32 = nc.inline_tensor(sel32_np().astype(BF), "c_e32")

    with ExitStack() as E0:
        c = BC(nc, E0.enter_context)
        nsb = E0.enter_context(nc.sbuf_tensor("nsb", [128, NFT * T], BF16))
        consts = {}
        for nm, h, shp, dt in (("ones_c", ones_c, [128, 1], BF16),
                               ("ones_r", ones_r, [1, 128], F32),
                               ("rotT", rotT, [128, 128], BF16),
                               ("e32", e32, [32, 32 * 128], BF16)):
            consts[nm] = E0.enter_context(nc.sbuf_tensor("cs_" + nm, shp, dt))
        cs_sb = E0.enter_context(nc.sbuf_tensor("cs_trig", [128, 2 * T], BF16))
        scs = E0.enter_context(nc.sbuf_tensor("scs", [32, QKV_W], BF16))

        def load_consts(sy):
            for nm, h in (("ones_c", ones_c), ("ones_r", ones_r), ("rotT", rotT), ("e32", e32)):
                c.tick(sy.dma_start(out=consts[nm][:], in_=h.ap()), "qs", 16)
        c.on("qs", load_consts)
        def load_consts2(g):
            c.tick(g.dma_start(out=cs_sb[:, 0:T], in_=cosT), "qg", 16)
            c.tick(g.dma_start(out=cs_sb[:, T:2 * T], in_=sinT), "qg", 16)
            c.tick(g.dma_start(out=scs[:], in_=sc), "qg", 16)
        c.on("pool", load_consts2)

        emit_ln(c, E0, xT, nsb, consts, rms=False, name="ln0")
        emit_qkv_rope(c, E0, nsb, qw, scs, cs_sb, consts, o_q, o_kv)

        def fin(g):
            c.wait(g, "qg")
        c.on("pool", fin)
    return nc


def emit_ln(c, E0, x_dram, nsb, consts, *, rms, name, nt=NFT, t=T, eps=LN_EPS,
            keep_x=None):
    nc = c.nc
    with ExitStack() as P:
        e = P.enter_context
        xf = e(nc.sbuf_tensor(name + "xf", [128, 2 * t], F32))
        sq = e(nc.sbuf_tensor(name + "sq", [128, 2 * t], BF16))
        st = e(nc.sbuf_tensor(name + "st", [1, 2 * t], F32))
        rr = e(nc.sbuf_tensor(name + "rr", [1, 2 * t], F32))
        rb = e(nc.sbuf_tensor(name + "rb", [128, 2 * t], BF16))
        ps = e(nc.psum_tensor(name + "ps", [1, 2 * t], F32))
        pb = e(nc.psum_tensor(name + "pb", [128, 2 * t], F32))
        xkeep = keep_x if keep_x is not None else nsb
        nh = t // 512

        lx = [c.xsem(name + "lx0"), c.xsem(name + "lx1")]
        base = {k: c.cnt[k] for k in c.cnt}

        def load(sy):
            for f in range(nt):
                if f >= 2:
                    sy.wait_ge(c.sem["dve"], base["dve"] + f - 1)
                c.tick(sy.dma_start(out=xf[:, (f % 2) * t:(f % 2 + 1) * t],
                                    in_=x_dram[f * 128:(f + 1) * 128, :]), lx[f % 2], 16)
        c.on("qs", load)

        def vcast(v):
            for f in range(nt):
                v.wait_ge(c.sem[lx[f % 2]], base[lx[f % 2]] + 16 * (f // 2 + 1))
                c.tick(v.tensor_copy(xkeep[:, f * t:(f + 1) * t],
                                     xf[:, (f % 2) * t:(f % 2 + 1) * t]), "dve", 1)
        c.on("dve", vcast)

        def asq(s):
            for f in range(nt):
                s.wait_ge(c.sem["dve"], base["dve"] + f + 1)
                if f >= 2:
                    s.wait_ge(c.sem["pe"], base["pe"] + f - 1)
                c.tick(s.activation(sq[:, (f % 2) * t:(f % 2 + 1) * t],
                                    xkeep[:, f * t:(f + 1) * t], AF.Square), "act", 1)
        c.on("act", asq)

        def pestat(tt):
            c.wait(tt, "qs", base["qs"])
            for f in range(nt):
                tt.wait_ge(c.sem["act"], base["act"] + f + 1)
                for h in range(nh):
                    if not rms:
                        tt.matmul(ps[0:1, h * 512:(h + 1) * 512], consts["ones_c"][:],
                                  xkeep[:, f * t + h * 512:f * t + (h + 1) * 512],
                                  start=(f == 0), stop=(f == nt - 1))
                    ii = tt.matmul(ps[0:1, t + h * 512:t + (h + 1) * 512], consts["ones_c"][:],
                                   sq[:, (f % 2) * t + h * 512:(f % 2) * t + (h + 1) * 512],
                                   start=(f == 0), stop=(f == nt - 1))
                c.tick(ii, "pe", 1)
        c.on("pe", pestat)

        D = float(nt * 128)
        def vstat(v):
            c.wait(v, "pe")
            if rms:
                c.tick(v.tensor_scalar(st[0:1, t:2 * t], ps[0:1, t:2 * t], 1.0 / D, eps,
                                       ALU.mult, ALU.add), "dve", 1)
            else:
                v.tensor_scalar_mul(st[0:1, 0:t], ps[0:1, 0:t], 1.0 / D)
                v.tensor_scalar_mul(st[0:1, t:2 * t], ps[0:1, t:2 * t], 1.0 / D)
                v.tensor_tensor(rr[0:1, 0:t], st[0:1, 0:t], st[0:1, 0:t], ALU.mult)
                v.tensor_tensor(st[0:1, t:2 * t], st[0:1, t:2 * t], rr[0:1, 0:t], ALU.subtract)
                c.tick(v.tensor_scalar_add(st[0:1, t:2 * t], st[0:1, t:2 * t], eps), "dve", 1)
        c.on("dve", vstat)

        def astat(s):
            c.wait(s, "dve")
            s.activation(rr[0:1, 0:t], st[0:1, t:2 * t], AF.Ln)
            c.tick(s.activation(rr[0:1, 0:t], rr[0:1, 0:t], AF.Exp, scale=-0.5), "act", 1)
        c.on("act", astat)

        def vstat2(v):
            c.wait(v, "act")
            if rms:
                c.tick(v.tensor_copy(rr[0:1, t:t + 1], rr[0:1, 0:1]), "dve", 1)
            else:
                c.tick(v.tensor_tensor(rr[0:1, t:2 * t], st[0:1, 0:t], rr[0:1, 0:t], ALU.mult),
                       "dve", 1)
        c.on("dve", vstat2)

        def peb(tt):
            c.wait(tt, "dve")
            nchunk = (t if rms else 2 * t) // 512
            for h in range(nchunk):
                ii = tt.matmul(pb[:, h * 512:(h + 1) * 512], consts["ones_r"][:],
                               rr[0:1, h * 512:(h + 1) * 512], start=True, stop=True)
            c.tick(ii, "pe", 1)
        c.on("pe", peb)

        def vb(v):
            c.wait(v, "pe")
            c.tick(v.tensor_copy(rb[:, 0:(t if rms else 2 * t)],
                                 pb[:, 0:(t if rms else 2 * t)]), "dve", 1)
            for f in range(nt):
                v.tensor_tensor(nsb[:, f * t:(f + 1) * t], xkeep[:, f * t:(f + 1) * t],
                                rb[:, 0:t], ALU.mult)
                if not rms:
                    ii = v.tensor_tensor(nsb[:, f * t:(f + 1) * t], nsb[:, f * t:(f + 1) * t],
                                         rb[:, t:2 * t], ALU.subtract)
                else:
                    ii = None
            if ii is None:
                ii = v.tensor_copy(rr[0:1, t:t + 1], rr[0:1, 0:1])
            c.tick(ii, "dve", 1)
        c.on("dve", vb)


def emit_qkv_rope(c, E0, nsb, qw_dram, scs, cs_sb, consts, o_q, o_kv):
    nc = c.nc
    OC = 256
    NOC = QKV_W // OC
    c.barrier_into("pe", "qg")
    c.barrier_into("dve", "qg")
    with ExitStack() as P:
        e = P.enter_context
        qwi = e(nc.sbuf_tensor("qwi", [128, 2 * 32 * OC], I8))
        qf = e(nc.sbuf_tensor("qf", [128, 32 * OC], BF16))
        sbc = e(nc.sbuf_tensor("sbc", [128, 32 * OC], BF16))
        og = e(nc.sbuf_tensor("og", [128, 4 * T], BF16))
        rt1 = e(nc.sbuf_tensor("rt1", [128, T], BF16))
        rt2 = e(nc.sbuf_tensor("rt2", [128, T], BF16))
        pod = e(nc.psum_tensor("pod", [128, 2 * T], F32))
        psb = e(nc.psum_tensor("psb", [128, 2 * 512], F32))
        pro = e(nc.psum_tensor("pro", [128, T], F32))

        lw = [c.xsem("qkvw0"), c.xsem("qkvw1")]
        for oc in range(NOC):
            base = {k: c.cnt[k] for k in c.cnt}
            buf = oc % 2
            def load(sy, oc=oc, buf=buf, base=base):
                if oc >= 2:
                    c.wait(sy, "act", base["act"] - 1)
                else:
                    c.wait(sy, "dve", base["dve"])
                    c.wait(sy, "act", base["act"])
                    c.wait(sy, "pe", base["pe"])
                src = qw_dram[:, oc * OC:(oc + 1) * OC].rearrange("(g p) w -> p g w", p=128)
                dst = qwi[:, buf * 32 * OC:(buf + 1) * 32 * OC].rearrange("p (g w) -> p g w", w=OC)
                c.tick(sy.dma_start(out=dst, in_=src), lw[buf], 16)
            c.on("qs", load)

            def cast(s, oc=oc, buf=buf, base=base):
                c.wait(s, lw[buf], base[lw[buf]] + 16)
                c.wait(s, "pe", base["pe"])
                c.tick(s.activation(qf[:], qwi[:, buf * 32 * OC:(buf + 1) * 32 * OC],
                                    AF.Copy, bias=-8.0), "act", 1)
            c.on("act", cast)

            def peb(tt, oc=oc, base=base):
                c.wait(tt, "dve", base["dve"])
                for kt in range(32):
                    if kt >= 2:
                        tt.wait_ge(c.sem["dve"], base["dve"] + (kt - 1))
                    ii = tt.matmul(psb[:, (kt % 2) * 512:(kt % 2) * 512 + OC],
                                   consts["e32"][:, kt * 128:(kt + 1) * 128],
                                   scs[:, oc * OC:(oc + 1) * OC], start=True, stop=True)
                    c.tick(ii, "pe", 1)
            c.on("pe", peb)

            def vsb(v, oc=oc, base=base):
                for kt in range(32):
                    v.wait_ge(c.sem["pe"], base["pe"] + kt + 1)
                    c.tick(v.tensor_copy(sbc[:, kt * OC:(kt + 1) * OC],
                                         psb[:, (kt % 2) * 512:(kt % 2) * 512 + OC]), "dve", 1)
            c.on("dve", vsb)

            def vdq(v, base=base):
                c.wait(v, "act")
                c.wait(v, "dve")
                c.tick(v.tensor_tensor(qf[:], qf[:], sbc[:], ALU.mult), "dve", 1)
            c.on("dve", vdq)

            def pemm(tt, oc=oc, base=base):
                c.wait(tt, "dve")
                for od in range(2):
                    for kt in range(32):
                        lhsT = qf[:, kt * OC + od * 128: kt * OC + od * 128 + 128]
                        for h in range(2):
                            ii = tt.matmul(pod[:, od * T + h * 512:od * T + (h + 1) * 512],
                                           lhsT, nsb[:, kt * T + h * 512:kt * T + (h + 1) * 512],
                                           start=(kt == 0), stop=(kt == 31))
                c.tick(ii, "pe", 1)
            c.on("pe", pemm)

            for od in range(2):
                gt = oc * 2 + od
                slot = gt % 4
                is_q = gt < 32
                is_k = 32 <= gt < 40
                def evac(v, od=od, slot=slot, base=base):
                    c.wait(v, "pe")
                    c.wait(v, "qg")
                    c.tick(v.tensor_copy(og[:, slot * T:(slot + 1) * T],
                                         pod[:, od * T:(od + 1) * T]), "dve", 1)
                c.on("dve", evac)

                if is_q or is_k:
                    def rope_mm(tt, slot=slot):
                        c.wait(tt, "dve")
                        for h in range(2):
                            ii = tt.matmul(pro[:, h * 512:(h + 1) * 512], consts["rotT"][:],
                                           og[:, slot * T + h * 512:slot * T + (h + 1) * 512],
                                           start=True, stop=True)
                        c.tick(ii, "pe", 1)
                    c.on("pe", rope_mm)

                    def rope_v(v, slot=slot):
                        c.wait(v, "pe")
                        v.tensor_tensor(rt1[:], og[:, slot * T:(slot + 1) * T],
                                        cs_sb[:, 0:T], ALU.mult)
                        v.tensor_tensor(rt2[:], pro[:, 0:T], cs_sb[:, T:2 * T], ALU.mult)
                        c.tick(v.tensor_tensor(og[:, slot * T:(slot + 1) * T],
                                               rt1[:], rt2[:], ALU.add), "dve", 1)
                    c.on("dve", rope_v)

                def store(g, gt=gt):
                    c.wait(g, "dve")
                    dst = (o_q[gt * 128:(gt + 1) * 128, :] if gt < 32
                           else o_kv[(gt - 32) * 128:(gt - 31) * 128, :])
                    c.tick(g.dma_start(out=dst, in_=og[:, (gt % 4) * T:(gt % 4 + 1) * T]),
                           "qg", 16)
                c.on("pool", store)




INVSQ = float(1.0 / np.sqrt(128.0))

def masks01_np():
    m = np.zeros((4, 128, 512), dtype=np.float32)
    for i in range(4):
        k = np.arange(128)[:, None] + 128 * i
        q = np.arange(512)[None, :]
        m[i] = (q >= k).astype(np.float32)
    return m

def build_B():
    nc = bass.Bass("TRN2", target_bir_lowering=False)
    qg = nc.dram_tensor("qg", [1024, S], BF16, kind="ExternalInput").ap()
    kvg = nc.dram_tensor("kvg", [512, S], BF16, kind="ExternalInput").ap()
    o_ctx = nc.dram_tensor("ctxu", [1024, S], BF16, kind="ExternalOutput").ap()
    rdr = nc.dram_tensor("rdr", [4, 512], F32, kind="Internal").ap()
    o_dbg = nc.dram_tensor("dbg", [128, 2048], F32, kind="ExternalOutput").ap()
    o_dbg2 = nc.dram_tensor("dbg2", [4, 512], F32, kind="ExternalOutput").ap()
    o_dbg3 = nc.dram_tensor("dbg3", [128, 2048], F32, kind="ExternalOutput").ap()

    ones_c = nc.inline_tensor(np.ones((128, 1), dtype=BF), "c_ones_c")
    ones_r = nc.inline_tensor(np.ones((1, 128), dtype=np.float32), "c_ones_r")
    identb = nc.inline_tensor(np.eye(128, dtype=BF), "c_identb")
    mask01 = nc.inline_tensor(np.ascontiguousarray(
        masks01_np().transpose(1, 0, 2).reshape(128, 4 * 512)).astype(BF), "c_mask01")
    e4 = np.zeros((128, 16), dtype=np.float32)
    for a in range(4):
        e4[:, 4 * a + a] = 1.0
    e4c = nc.inline_tensor(e4.astype(BF), "c_e4")

    with ExitStack() as E0:
        c = BC(nc, E0.enter_context)
        consts = {}
        for nm, h, shp, dt in (("ones_c", ones_c, [128, 1], BF16),
                               ("ones_r", ones_r, [1, 128], F32),
                               ("identb", identb, [128, 128], BF16),
                               ("mask01", mask01, [128, 4 * 512], BF16),
                               ("e4", e4c, [128, 16], BF16)):
            consts[nm] = E0.enter_context(nc.sbuf_tensor("cs_" + nm, shp, dt))
        def load_consts(sy):
            for nm, h in (("ones_c", ones_c), ("ones_r", ones_r), ("identb", identb),
                          ("mask01", mask01), ("e4", e4c)):
                c.tick(sy.dma_start(out=consts[nm][:], in_=h.ap()), "qs", 16)
        c.on("qs", load_consts)

        q4 = E0.enter_context(nc.sbuf_tensor("q4", [128, 4 * S], BF16))
        ksb = E0.enter_context(nc.sbuf_tensor("ksb", [128, S], BF16))
        vsb = E0.enter_context(nc.sbuf_tensor("vsb", [128, S], BF16))
        vtm = E0.enter_context(nc.sbuf_tensor("vtm", [128, S], BF16))
        sqs = E0.enter_context(nc.sbuf_tensor("sqs", [128, S], BF16))
        exps = E0.enter_context(nc.sbuf_tensor("exps", [128, 2048], BF16))
        brow = E0.enter_context(nc.sbuf_tensor("brow", [1, 16], F32))
        cbs = E0.enter_context(nc.sbuf_tensor("cbs", [128, 4], F32))
        rs = E0.enter_context(nc.sbuf_tensor("rs", [4, 512], F32))
        rrow = E0.enter_context(nc.sbuf_tensor("rrow", [1, 2048], F32))
        rb = E0.enter_context(nc.sbuf_tensor("rb", [128, 512], BF16))
        ost = E0.enter_context(nc.sbuf_tensor("ost", [128, 2048], BF16))

        ps_s = E0.enter_context(nc.psum_tensor("ps_s", [128, 1024], F32))
        ps_c = E0.enter_context(nc.psum_tensor("ps_c", [128, 2048], F32))
        ps_u = E0.enter_context(nc.psum_tensor("ps_u", [4, 512], F32))
        ps_x = E0.enter_context(nc.psum_tensor("ps_x", [128, 512], F32))

        lq = c.xsem("lq")
        lkv = c.xsem("lkv")
        PSX = ps_x[:, 0:64].bitcast(BF16)

        for u in range(2):
            ub = {k: c.cnt[k] for k in c.cnt}
            def load1(sy, u=u, ub=ub):
                c.wait(sy, "pe", ub["pe"])
                c.wait(sy, "dve", ub["dve"])
                c.wait(sy, "act", ub["act"])
                c.tick(sy.dma_start(out=ksb[:], in_=kvg[u * 128:(u + 1) * 128, :]), lkv, 16)
                c.tick(sy.dma_start(out=vsb[:], in_=kvg[256 + u * 128:256 + (u + 1) * 128, :]), lkv, 16)
                for qh in range(4):
                    c.tick(sy.dma_start(out=q4[:, qh * S:(qh + 1) * S],
                                        in_=qg[u * 512 + qh * 128:u * 512 + (qh + 1) * 128, :]),
                           lq, 16)
            c.on("qs", load1)

            def pet(tt, u=u, ub=ub):
                c.wait(tt, lkv, ub[lkv] + 32)
                c.wait(tt, "qs")
                for kt in range(32):
                    if kt >= 1:
                        tt.wait_ge(c.sem["dve"], ub["dve"] + kt)
                    c.tick(tt.transpose(PSX, vsb[:, kt * 128:(kt + 1) * 128],
                                        consts["identb"][:]), "pe", 1)
            c.on("pe", pet)
            def vet(v, u=u, ub=ub):
                for kt in range(32):
                    v.wait_ge(c.sem["pe"], ub["pe"] + kt + 1)
                    c.tick(v.tensor_copy(vtm[:, kt * 128:(kt + 1) * 128], PSX), "dve", 1)
            c.on("dve", vet)

            def act_b(s, u=u, ub=ub):
                c.wait(s, lkv, ub[lkv] + 32)
                c.tick(s.activation(sqs[:], ksb[:], AF.Square), "act", 1)
                c.wait(s, lq, ub[lq] + 64)
                for qh in range(4):
                    c.wait(s, "pe", ub["pe"] + 32 + 8 * (qh + 1))
                    c.tick(s.activation(sqs[:], q4[:, qh * S:(qh + 1) * S], AF.Square), "act", 1)
            c.on("act", act_b)

            def pe_b(tt, u=u, ub=ub):
                for src_i in range(5):
                    c.wait(tt, "act", ub["act"] + src_i + 1)
                    for h8 in range(8):
                        c.wait(tt, "dve", ub["dve"] + 32 + src_i * 8 + h8)
                        c.tick(tt.matmul(ps_x[0:1, 0:512], consts["ones_c"][:],
                                         sqs[:, h8 * 512:(h8 + 1) * 512],
                                         start=True, stop=True), "pe", 1)
            c.on("pe", pe_b)
            def v_b(v, u=u, ub=ub):
                for src_i in range(5):
                    for h8 in range(8):
                        v.wait_ge(c.sem["pe"], ub["pe"] + 32 + src_i * 8 + h8 + 1)
                        if h8 == 0:
                            c.tick(v.reduce_max(brow[0:1, src_i:src_i + 1], ps_x[0:1, 0:512],
                                                axis=AX.X), "dve", 1)
                        else:
                            v.reduce_max(brow[0:1, 6:7], ps_x[0:1, 0:512], axis=AX.X)
                            c.tick(v.tensor_max(brow[0:1, src_i:src_i + 1],
                                                brow[0:1, src_i:src_i + 1],
                                                brow[0:1, 6:7]), "dve", 1)
            c.on("dve", v_b)

            def v_bias(v, u=u):
                v.reduce_max(brow[0:1, 5:6], brow[0:1, 1:5], axis=AX.X)
                c.tick(v.tensor_scalar(brow[0:1, 8:9], brow[0:1, 5:6], brow[0:1, 0:1], None,
                                       ALU.mult), "dve", 1)
            c.on("dve", v_bias)
            def a_bias(s, u=u):
                c.wait(s, "dve")
                s.activation(brow[0:1, 8:9], brow[0:1, 8:9], AF.Ln)
                c.tick(s.activation(brow[0:1, 8:9], brow[0:1, 8:9], AF.Exp, scale=0.5), "act", 1)
            c.on("act", a_bias)
            def v_bias2(v, u=u):
                c.wait(v, "act")
                c.tick(v.tensor_scalar(brow[0:1, 8:9], brow[0:1, 8:9], -INVSQ, 60.0,
                                       ALU.mult, ALU.add), "dve", 1)
            c.on("dve", v_bias2)
            def pe_bias(tt, u=u):
                c.wait(tt, "dve")
                c.tick(tt.matmul(ps_x[:, 0:1], consts["ones_r"][:], brow[0:1, 8:9],
                                 start=True, stop=True), "pe", 1)
            c.on("pe", pe_bias)
            def v_bias3(v, u=u):
                c.wait(v, "pe")
                c.tick(v.tensor_copy(cbs[:, 0:1], ps_x[:, 0:1]), "dve", 1)
            c.on("dve", v_bias3)

            for qc in range(8):
                K_ = 4 * qc + 4
                B0 = {k: c.cnt[k] for k in c.cnt}
                def peloop(tt, u=u, qc=qc, K_=K_, B0=B0):
                    c.wait(tt, "dve", B0["dve"])
                    for i in range(K_):
                        kt = i
                        diag = (kt // 4 == qc)
                        if i > 0:
                            tt.wait_ge(c.sem["act"], B0["act"] + 2 * i)
                        for qh in (0, 1):
                            ii = tt.matmul(ps_s[:, qh * 512:(qh + 1) * 512],
                                           ksb[:, kt * 128:(kt + 1) * 128],
                                           q4[:, qh * S + qc * 512: qh * S + (qc + 1) * 512],
                                           start=True, stop=True)
                        c.tick(ii, "pe", 1)
                        tt.wait_ge(c.sem["act"], B0["act"] + 2 * i + 1)
                        for qh in (2, 3):
                            ii = tt.matmul(ps_s[:, (qh - 2) * 512:(qh - 1) * 512],
                                           ksb[:, kt * 128:(kt + 1) * 128],
                                           q4[:, qh * S + qc * 512: qh * S + (qc + 1) * 512],
                                           start=True, stop=True)
                        c.tick(ii, "pe", 1)
                        tt.wait_ge(c.sem["act"], B0["act"] + 2 * i + 2)
                        if diag:
                            tt.wait_ge(c.sem["dve"], B0["dve"] + 4 * (kt - 4 * qc) + 4)
                        for qh in range(4):
                            tt.matmul(ps_c[:, qh * 512:(qh + 1) * 512],
                                      vtm[:, kt * 128:(kt + 1) * 128],
                                      exps[:, qh * 512:(qh + 1) * 512],
                                      start=(i == 0), stop=(i == K_ - 1))
                        for qh in range(4):
                            ii = tt.matmul(ps_u[:], consts["e4"][:, 4 * qh:4 * qh + 4],
                                           exps[:, qh * 512:(qh + 1) * 512],
                                           start=(i == 0 and qh == 0),
                                           stop=(i == K_ - 1 and qh == 3))
                        c.tick(ii, "pe", 1)
                c.on("pe", peloop)

                def actloop(s, u=u, qc=qc, K_=K_, B0=B0):
                    for i in range(K_):
                        s.wait_ge(c.sem["pe"], B0["pe"] + 3 * i + 1)
                        c.tick(s.activation(exps[:, 0:1024], ps_s[:], AF.Exp,
                                            bias=cbs[:, 0:1], scale=INVSQ), "act", 1)
                        s.wait_ge(c.sem["pe"], B0["pe"] + 3 * i + 2)
                        c.tick(s.activation(exps[:, 1024:2048], ps_s[:], AF.Exp,
                                            bias=cbs[:, 0:1], scale=INVSQ), "act", 1)
                c.on("act", actloop)
                def dveloop(v, u=u, qc=qc, K_=K_, B0=B0):
                    for i in range(K_):
                        kt = i
                        if kt // 4 == qc:
                            m = kt - 4 * qc
                            v.wait_ge(c.sem["act"], B0["act"] + 2 * i + 2)
                            for qh in range(4):
                                c.tick(v.tensor_tensor(exps[:, qh * 512:(qh + 1) * 512],
                                                       exps[:, qh * 512:(qh + 1) * 512],
                                                       consts["mask01"][:, m * 512:(m + 1) * 512],
                                                       ALU.mult), "dve", 1)
                c.on("dve", dveloop)

                def v_r(v, u=u, qc=qc, B0=B0):
                    c.wait(v, "pe")
                    c.tick(v.reciprocal(rs[:], ps_u[:]), "dve", 1)
                c.on("dve", v_r)
                def g_r(g, u=u, qc=qc):
                    c.wait(g, "dve")
                    c.tick(g.dma_start(out=rdr, in_=rs[:]), "qg", 16)
                    c.wait(g, "qg")
                    c.tick(g.dma_start(out=rrow[0:1, :],
                                       in_=rdr.rearrange("a b -> (a b)").unsqueeze(0)), "qg", 16)
                c.on("pool", g_r)
                for qh in range(4):
                    def pe_rb(tt, u=u, qc=qc, qh=qh):
                        c.wait(tt, "qg")
                        c.wait(tt, "dve")
                        c.tick(tt.matmul(ps_x[:, 0:512], consts["ones_r"][:],
                                         rrow[0:1, qh * 512:(qh + 1) * 512],
                                         start=True, stop=True), "pe", 1)
                    c.on("pe", pe_rb)
                    def v_rb(v, u=u, qc=qc, qh=qh):
                        c.wait(v, "pe")
                        v.tensor_copy(rb[:], ps_x[:, 0:512])
                        c.tick(v.tensor_tensor(ost[:, qh * 512:(qh + 1) * 512],
                                               ps_c[:, qh * 512:(qh + 1) * 512],
                                               rb[:], ALU.mult), "dve", 1)
                    c.on("dve", v_rb)
                def g_dbg(g, u=u, qc=qc):
                    if u == 0 and qc == 0:
                        c.wait(g, "dve")
                        c.tick(g.dma_start(out=o_dbg2, in_=rs[:]), "qg", 16)
                        c.tick(g.dma_start(out=o_dbg[:, 0:1], in_=cbs[:, 0:1], allow_slow_non_contiguous=True), "qg", 16)
                        c.tick(g.dma_start(out=o_dbg[0:1, 4:20], in_=brow[0:1, :]), "qg", 16)
                        c.tick(g.dma_start(out=o_dbg3[:, 0:2048], in_=exps[:]), "qg", 16)
                c.on("pool", g_dbg)
                def g_st(g, u=u, qc=qc):
                    c.wait(g, "dve")
                    for qh in range(4):
                        c.tick(g.dma_start(
                            out=o_ctx[u * 512 + qh * 128:u * 512 + (qh + 1) * 128,
                                      qc * 512:(qc + 1) * 512],
                            in_=ost[:, qh * 512:(qh + 1) * 512]), "qg", 16)
                c.on("pool", g_st)

        def fin(g):
            c.wait(g, "qg")
        c.on("pool", fin)
    return nc


def build_B2():
    nc = bass.Bass("TRN2", target_bir_lowering=False)
    xT = nc.dram_tensor("xT", [DM, T], F32, kind="ExternalInput").ap()
    ctxT = nc.dram_tensor("ctxT", [DM, T], BF16, kind="ExternalInput").ap()
    qwo = nc.dram_tensor("qwo", [DM, DM], I8, kind="ExternalInput").ap()
    sco = nc.dram_tensor("sco", [32, DM], F32, kind="ExternalInput").ap()
    o_x1 = nc.dram_tensor("x1T", [DM, T], F32, kind="ExternalOutput").ap()

    ones_r = nc.inline_tensor(np.ones((1, 128), dtype=np.float32), "c_ones_r")
    e32 = nc.inline_tensor(sel32_np().astype(BF), "c_e32")

    with ExitStack() as E0:
        c = BC(nc, E0.enter_context)
        consts = {}
        for nm, h, shp, dt in (("ones_r", ones_r, [1, 128], F32),
                               ("e32", e32, [32, 32 * 128], BF16)):
            consts[nm] = E0.enter_context(nc.sbuf_tensor("cs_" + nm, shp, dt))
        scs = E0.enter_context(nc.sbuf_tensor("scs", [32, DM], BF16))
        csb = E0.enter_context(nc.sbuf_tensor("csb", [128, NFT * T], BF16))
        def load_consts(sy):
            for nm, h in (("ones_r", ones_r), ("e32", e32)):
                c.tick(sy.dma_start(out=consts[nm][:], in_=h.ap()), "qs", 16)
            for f in range(NFT):
                c.tick(sy.dma_start(out=csb[:, f * T:(f + 1) * T],
                                    in_=ctxT[f * 128:(f + 1) * 128, :]), "qs", 16)
        c.on("qs", load_consts)
        def load_scs(g):
            c.tick(g.dma_start(out=scs[:], in_=sco), "qg", 16)
        c.on("pool", load_scs)

        emit_dqmm_resid(c, E0, csb, qwo, scs, xT, o_x1, consts, width=DM)

        def fin(g):
            c.wait(g, "qg")
        c.on("pool", fin)
    return nc


def emit_dqmm_resid(c, E0, rhs_sb, qw_dram, scs, x_dram, out_dram, consts, width):
    nc = c.nc
    OC = 256
    NOC = width // OC
    c.barrier_into("pe", "qg")
    c.barrier_into("dve", "qg")
    with ExitStack() as P:
        e = P.enter_context
        qwi = e(nc.sbuf_tensor("oqwi", [128, 2 * 32 * OC], I8))
        qf = e(nc.sbuf_tensor("oqf", [128, 32 * OC], BF16))
        sbc = e(nc.sbuf_tensor("osbc", [128, 32 * OC], BF16))
        xf = e(nc.sbuf_tensor("oxf", [128, 2 * 2 * T], F32))
        og = e(nc.sbuf_tensor("oog", [128, 2 * 2 * T], F32))
        pod = e(nc.psum_tensor("opod", [128, 2 * T], F32))
        psb = e(nc.psum_tensor("opsb", [128, 2 * 512], F32))
        lw = [c.xsem("ow0"), c.xsem("ow1")]
        lx = [c.xsem("ox0"), c.xsem("ox1")]

        for oc in range(NOC):
            base = {k: c.cnt[k] for k in c.cnt}
            buf = oc % 2
            def load(sy, oc=oc, buf=buf, base=base):
                if oc >= 2:
                    c.wait(sy, "act", base["act"] - 1)
                else:
                    c.wait(sy, "dve", base["dve"])
                    c.wait(sy, "act", base["act"])
                    c.wait(sy, "pe", base["pe"])
                src = qw_dram[:, oc * OC:(oc + 1) * OC].rearrange("(g p) w -> p g w", p=128)
                dst = qwi[:, buf * 32 * OC:(buf + 1) * 32 * OC].rearrange("p (g w) -> p g w", w=OC)
                c.tick(sy.dma_start(out=dst, in_=src), lw[buf], 16)
                if oc >= 2:
                    c.wait(sy, "dve", base["dve"])
                for od in range(2):
                    gt = 2 * oc + od
                    c.tick(sy.dma_start(out=xf[:, (buf * 2 + od) * T:(buf * 2 + od + 1) * T],
                                        in_=x_dram[gt * 128:(gt + 1) * 128, :]), lx[buf], 16)
            c.on("qs", load)

            def cast(s, oc=oc, buf=buf, base=base):
                c.wait(s, lw[buf], base[lw[buf]] + 16)
                c.wait(s, "pe", base["pe"])
                c.tick(s.activation(qf[:], qwi[:, buf * 32 * OC:(buf + 1) * 32 * OC],
                                    AF.Copy, bias=-8.0), "act", 1)
            c.on("act", cast)

            def peb(tt, oc=oc, base=base):
                c.wait(tt, "dve", base["dve"])
                for kt in range(32):
                    if kt >= 2:
                        tt.wait_ge(c.sem["dve"], base["dve"] + (kt - 1))
                    ii = tt.matmul(psb[:, (kt % 2) * 512:(kt % 2) * 512 + OC],
                                   consts["e32"][:, kt * 128:(kt + 1) * 128],
                                   scs[:, oc * OC:(oc + 1) * OC], start=True, stop=True)
                    c.tick(ii, "pe", 1)
            c.on("pe", peb)
            def vsb(v, oc=oc, base=base):
                for kt in range(32):
                    v.wait_ge(c.sem["pe"], base["pe"] + kt + 1)
                    c.tick(v.tensor_copy(sbc[:, kt * OC:(kt + 1) * OC],
                                         psb[:, (kt % 2) * 512:(kt % 2) * 512 + OC]), "dve", 1)
            c.on("dve", vsb)
            def vdq(v, base=base):
                c.wait(v, "act")
                c.tick(v.tensor_tensor(qf[:], qf[:], sbc[:], ALU.mult), "dve", 1)
            c.on("dve", vdq)

            def pemm(tt, oc=oc, base=base):
                c.wait(tt, "dve")
                for od in range(2):
                    for kt in range(32):
                        lhsT = qf[:, kt * OC + od * 128: kt * OC + od * 128 + 128]
                        for h in range(2):
                            ii = tt.matmul(pod[:, od * T + h * 512:od * T + (h + 1) * 512],
                                           lhsT, rhs_sb[:, kt * T + h * 512:kt * T + (h + 1) * 512],
                                           start=(kt == 0), stop=(kt == 31))
                c.tick(ii, "pe", 1)
            c.on("pe", pemm)

            for od in range(2):
                gt = 2 * oc + od
                def evac(v, od=od, oc=oc, buf=buf, base=base):
                    c.wait(v, "pe")
                    c.wait(v, lx[buf], base[lx[buf]] + 32)
                    c.wait(v, "qg")
                    c.tick(v.tensor_tensor(og[:, (buf * 2 + od) * T:(buf * 2 + od + 1) * T],
                                           pod[:, od * T:(od + 1) * T],
                                           xf[:, (buf * 2 + od) * T:(buf * 2 + od + 1) * T],
                                           ALU.add), "dve", 1)
                c.on("dve", evac)
                def store(g, gt=gt, od=od, buf=buf):
                    c.wait(g, "dve")
                    c.tick(g.dma_start(out=out_dram[gt * 128:(gt + 1) * 128, :],
                                       in_=og[:, (buf * 2 + od) * T:(buf * 2 + od + 1) * T]),
                           "qg", 16)
                c.on("pool", store)


RMS_EPS = 1.1920929e-07

class _StageDone(Exception):
    pass

def build_C(stage=99):
    nc = bass.Bass("TRN2", target_bir_lowering=False)
    xT = nc.dram_tensor("xT", [DM, T], F32, kind="ExternalInput").ap()
    wcat = nc.dram_tensor("wcat", [DM, 24], F32, kind="ExternalInput").ap()
    bcat = nc.dram_tensor("bcat", [1, 25], F32, kind="ExternalInput").ap()
    wbr = nc.dram_tensor("wbr", [1024, 1024], F32, kind="ExternalInput").ap()
    o_x2 = nc.dram_tensor("x2T", [DM, T], F32, kind="ExternalOutput").ap()
    o_cdbg = nc.dram_tensor("cdbg", [128, 512], F32, kind="ExternalOutput").ap()
    o_cdbg2 = nc.dram_tensor("cdbg2", [128, 128], F32, kind="ExternalOutput").ap()
    o_cdbg3 = nc.dram_tensor("cdbg3", [128, 256], F32, kind="ExternalOutput").ap()
    o_cdbg4 = nc.dram_tensor("cdbg4", [128, 64], F32, kind="ExternalOutput").ap()

    ones_c = nc.inline_tensor(np.ones((128, 1), dtype=BF), "c_ones_c")
    ones_r = nc.inline_tensor(np.ones((1, 128), dtype=np.float32), "c_ones_r")
    identf = nc.inline_tensor(np.eye(128, dtype=np.float32), "c_identf")
    i16 = nc.inline_tensor(np.eye(4, dtype=np.float32).reshape(1, 16), "c_i16")

    with ExitStack() as E0:
        c = BC(nc, E0.enter_context)
        consts = {}
        for nm, h, shp, dt in (("ones_c", ones_c, [128, 1], BF16),
                               ("ones_r", ones_r, [1, 128], F32),
                               ("identf", identf, [128, 128], F32),
                               ("i16", i16, [1, 16], F32)):
            consts[nm] = E0.enter_context(nc.sbuf_tensor("cs_" + nm, shp, dt))
        nsb = E0.enter_context(nc.sbuf_tensor("nsb", [128, NFT * T], BF16))
        lg = c.xsem("lgC")

        def load_k(sy):
            for nm, h in (("ones_c", ones_c), ("ones_r", ones_r), ("identf", identf),
                          ("i16", i16)):
                c.tick(sy.dma_start(out=consts[nm][:], in_=h.ap()), "qs", 16)
        c.on("qs", load_k)

        emit_ln(c, E0, xT, nsb, consts, rms=False, name="lnc")

        if True:
          with ExitStack() as P:
            e = P.enter_context
            wsb = e(nc.sbuf_tensor("wsb", [128, 32 * 24], BF16))
            wbrs = e(nc.sbuf_tensor("wbrs", [128, 8 * 1024], BF16))
            brow = e(nc.sbuf_tensor("brow", [1, 32], F32))
            logt = e(nc.sbuf_tensor("logt", [24, T], F32))
            ltm = e(nc.sbuf_tensor("ltm", [128, 8 * 32], F32))
            uv = e(nc.sbuf_tensor("uv", [128, 8 * 8], F32))
            sc1 = e(nc.sbuf_tensor("sc1", [128, 8 * 16], F32))
            sc2 = e(nc.sbuf_tensor("sc2", [128, 8 * 8], F32))
            rstd_r = e(nc.sbuf_tensor("rstd_r", [1, T], F32))
            rstd_t = e(nc.sbuf_tensor("rstd_t", [128, 8], F32))
            bb = e(nc.sbuf_tensor("bbx", [128, 24], F32))
            gi = e(nc.sbuf_tensor("gi", [128, 16], F32))
            gcol = e(nc.sbuf_tensor("gcol", [128, 4], F32))
            hb = e(nc.sbuf_tensor("hbx", [128, 20 * T], BF16))
            hrow = e(nc.sbuf_tensor("hrow", [1, T], F32))
            xpre = e(nc.sbuf_tensor("xpre", [128, 8 * T], BF16))
            ysb = e(nc.sbuf_tensor("ysb", [128, 8 * T], BF16))
            t1s = e(nc.sbuf_tensor("t1s", [128, T], F32))
            t2s = e(nc.sbuf_tensor("t2s", [128, T], F32))
            t3s = e(nc.sbuf_tensor("t3s", [128, T], F32))
            xf2 = e(nc.sbuf_tensor("xf2", [128, 2 * T], F32))
            out2 = e(nc.sbuf_tensor("out2", [128, 2 * T], F32))
            L0_ = c.cnt[lg]
            if stage <= 0:
                c.enabled = False
            if stage == -2:
                c.enabled = False
            def load_g2(g):
                c.wait(g, "dve")
                c.wait(g, "pe")
                c.tick(g.dma_start(out=brow[0:1, 0:25], in_=bcat), lg, 16)
                c.tick(g.dma_start(out=wsb[:],
                                   in_=wcat.rearrange("(g p) w -> p g w", p=128)), lg, 16)
                for f8 in range(8):
                    c.tick(g.dma_start(out=wbrs[:, f8 * 1024:(f8 + 1) * 1024],
                                       in_=wbr[f8 * 128:(f8 + 1) * 128, :]), lg, 16)
            c.on("pool", load_g2)
            if stage == -1:
                c.enabled = False
            sq = e(nc.sbuf_tensor("csq", [128, 2 * T], BF16))
            pstat = e(nc.psum_tensor("cps", [1, T], F32))
            plog = e(nc.psum_tensor("cplog", [24, T], F32))
            pxx = e(nc.psum_tensor("cpx", [128, 1024], F32))

            B = {k: c.cnt[k] for k in c.cnt}
            def asq(s):
                c.wait(s, "dve", B["dve"])
                for f in range(NFT):
                    if f >= 2:
                        s.wait_ge(c.sem["pe"], B["pe"] + f - 1)
                    c.tick(s.activation(sq[:, (f % 2) * T:(f % 2 + 1) * T],
                                        nsb[:, f * T:(f + 1) * T], AF.Square), "act", 1)
            c.on("act", asq)
            def pes(tt):
                c.wait(tt, lg, L0_ + 160)
                for f in range(NFT):
                    tt.wait_ge(c.sem["act"], B["act"] + f + 1)
                    for h in range(2):
                        tt.matmul(pstat[0:1, h * 512:(h + 1) * 512], consts["ones_c"][:],
                                  sq[:, (f % 2) * T + h * 512:(f % 2) * T + (h + 1) * 512],
                                  start=(f == 0), stop=(f == NFT - 1))
                    for h in range(2):
                        ii = tt.matmul(plog[:, h * 512:(h + 1) * 512], wsb[:, f * 24:(f + 1) * 24],
                                       nsb[:, f * T + h * 512:f * T + (h + 1) * 512],
                                       start=(f == 0), stop=(f == NFT - 1))
                    c.tick(ii, "pe", 1)
            c.on("pe", pes)
            def vst(v):
                v.memset(ltm[:], 0.0)
                c.wait(v, "pe")
                c.tick(v.tensor_scalar(rstd_r[0:1, :], pstat[0:1, :], 1.0 / DM, RMS_EPS,
                                       ALU.mult, ALU.add), "dve", 1)
                c.tick(v.tensor_copy(logt[:], plog[:]), "dve", 1)
            c.on("dve", vst)
            def ast(s):
                c.wait(s, "dve")
                s.activation(rstd_r[0:1, :], rstd_r[0:1, :], AF.Ln)
                c.tick(s.activation(rstd_r[0:1, :], rstd_r[0:1, :], AF.Exp, scale=-0.5), "act", 1)
            c.on("act", ast)

            if stage <= 1:
                c.enabled = False
            Bp = {k: c.cnt[k] for k in c.cnt}
            def pet(tt):
                c.wait(tt, "act")
                for t8 in range(8):
                    if t8 >= 1:
                        tt.wait_ge(c.sem["dve"], Bp["dve"] + 2 * t8)
                    c.tick(tt.transpose(pxx[:, 0:1], rstd_r[0:1, t8 * 128:(t8 + 1) * 128],
                                        consts["identf"][0:1, 0:1]), "pe", 1)
                    c.tick(tt.transpose(pxx[:, 512:536], logt[:, t8 * 128:(t8 + 1) * 128],
                                        consts["identf"][0:24, 0:24]), "pe", 1)
            c.on("pe", pet)
            def vtp(v):
                for t8 in range(8):
                    v.wait_ge(c.sem["pe"], Bp["pe"] + 2 * t8 + 2)
                    c.tick(v.tensor_copy(rstd_t[:, t8:t8 + 1], pxx[:, 0:1]), "dve", 1)
                    c.tick(v.tensor_copy(ltm[:, t8 * 32:t8 * 32 + 24], pxx[:, 512:536]), "dve", 1)
            c.on("dve", vtp)

            if stage <= 2:
                c.enabled = False
            Bq = {k: c.cnt[k] for k in c.cnt}
            def peb2(tt):
                c.wait(tt, lg, L0_ + 160)
                c.wait(tt, "dve")
                c.tick(tt.matmul(pxx[:, 0:24], consts["ones_r"][:], brow[0:1, 0:24],
                                 start=True, stop=True), "pe", 1)
                c.tick(tt.matmul(pxx[:, 32:48], consts["ones_r"][:], consts["i16"][:],
                                 start=True, stop=True), "pe", 1)
            c.on("pe", peb2)
            def vfin(v):
                c.wait(v, "pe")
                c.sedge(v, v.tensor_copy(bb[:], pxx[:, 0:24]))
                c.sedge(v, v.tensor_copy(gi[:], pxx[:, 32:48]))
                for t8 in range(8):
                    c.sedge(v, v.tensor_scalar_mul(ltm[:, t8 * 32:t8 * 32 + 24],
                                                   ltm[:, t8 * 32:t8 * 32 + 24],
                                                   rstd_t[:, t8:t8 + 1]))
                    c.sedge(v, v.tensor_scalar(ltm[:, t8 * 32:t8 * 32 + 16],
                                               ltm[:, t8 * 32:t8 * 32 + 16], 10.0, None,
                                               ALU.mult))
                    c.sedge(v, v.tensor_tensor(ltm[:, t8 * 32 + 16:t8 * 32 + 24],
                                               ltm[:, t8 * 32 + 16:t8 * 32 + 24],
                                               bb[:, 16:24], ALU.add))
                c.sedge(v, v.tensor_scalar_mul(sc1[:, 0:16], bb[:, 0:16], 10.0))
                for t8 in range(8):
                    c.sedge(v, v.tensor_tensor(ltm[:, t8 * 32:t8 * 32 + 16],
                                               ltm[:, t8 * 32:t8 * 32 + 16], sc1[:, 0:16],
                                               ALU.add))
                c.tick(v.memset(uv[:], 0.0), "dve", 1)
            c.on("dve", vfin)

            if stage <= 3:
                c.enabled = False
            def gdbg3(g):
                c.wait(g, "dve")
                c.tick(g.dma_start(out=o_cdbg3, in_=ltm[:]), "qg", 16)
                c.tick(g.dma_start(out=o_cdbg4[:, 0:24], in_=bb[:]), "qg", 16)
                c.tick(g.dma_start(out=o_cdbg4[:, 32:48], in_=gi[:]), "qg", 16)
                c.tick(g.dma_start(out=o_cdbg4[:, 48:49], in_=rstd_t[:, 0:1], allow_slow_non_contiguous=True), "qg", 16)
            c.on("pool", gdbg3)
            zbuf = e(nc.sbuf_tensor("zbuf", [128, 8 * 16], F32))
            ubuf = e(nc.sbuf_tensor("ubuf", [128, 8 * 4], F32))
            vbuf = e(nc.sbuf_tensor("vbuf", [128, 8 * 4], F32))
            mbuf = e(nc.sbuf_tensor("mbuf", [128, 8 * 4], F32))
            sbuf2 = e(nc.sbuf_tensor("sbuf2", [128, 8 * 4], F32))

            def zv(x):
                return x.rearrange("p (a i j) -> p a i j", i=4, j=4)
            def av(x):
                return x.rearrange("p (a w) -> p a w", w=4)

            def vz(v):
                for t8 in range(8):
                    v.tensor_copy(zbuf[:, 16 * t8:16 * t8 + 16], ltm[:, 32 * t8:32 * t8 + 16])
                v.memset(ubuf[:], 0.0)
                c.sedge(v, v.memset(vbuf[:], 0.0))
            c.on("dve", vz)

            NIT = 50
            D0 = c.cnt["dve"]
            A0 = c.cnt["act"]
            def sink_dve(v):
                for k in range(2 * NIT):
                    half = k % 2
                    if k == 0:
                        v.wait_ge(c.sem["dve"], D0)
                    if k > 0:
                        v.wait_ge(c.sem["act"], A0 + 2 * k)
                    if half == 0:
                        bias = av(vbuf[:]).unsqueeze(2).broadcast_to([128, 8, 4, 4])
                    else:
                        bias = av(ubuf[:]).unsqueeze(3).broadcast_to([128, 8, 4, 4])
                    c.sedge(v, v.tensor_tensor(zv(sc1[:, 0:128]), zv(zbuf[:]), bias, ALU.add))
                    if half == 0:
                        c.sedge(v, v.reduce_max(av(mbuf[:]), zv(sc1[:, 0:128]), axis=AX.X))
                        mb = av(mbuf[:]).unsqueeze(3).broadcast_to([128, 8, 4, 4])
                    else:
                        c.sedge(v, v.reduce_max(av(mbuf[:]),
                                                zv(sc1[:, 0:128]).transpose([0, 1, 3, 2]),
                                                axis=AX.X))
                        mb = av(mbuf[:]).unsqueeze(2).broadcast_to([128, 8, 4, 4])
                    c.tick(v.tensor_tensor(zv(sc1[:, 0:128]), zv(sc1[:, 0:128]), mb,
                                           ALU.subtract), "dve", 1)
                    v.wait_ge(c.sem["act"], A0 + 2 * k + 1)
                    if half == 0:
                        c.tick(v.reduce_sum(av(sbuf2[:]), zv(sc1[:, 0:128]), axis=AX.X),
                               "dve", 1)
                    else:
                        c.tick(v.reduce_sum(av(sbuf2[:]),
                                            zv(sc1[:, 0:128]).transpose([0, 1, 3, 2]),
                                            axis=AX.X), "dve", 1)
                    v.wait_ge(c.sem["act"], A0 + 2 * k + 2)
                    dst = ubuf[:] if half == 0 else vbuf[:]
                    c.sedge(v, v.tensor_tensor(dst, mbuf[:], sbuf2[:], ALU.add))
                    c.tick(v.tensor_scalar_mul(dst, dst, -1.0), "dve", 1)
            c.on("dve", sink_dve)
            def sink_act(s):
                for k in range(2 * NIT):
                    s.wait_ge(c.sem["dve"], D0 + 6 * k + 3)
                    c.tick(s.activation(sc1[:, 0:128], sc1[:, 0:128], AF.Exp), "act", 1)
                    s.wait_ge(c.sem["dve"], D0 + 6 * k + 4)
                    c.tick(s.activation(sbuf2[:], sbuf2[:], AF.Ln), "act", 1)
            c.on("act", sink_act)

            if stage <= 4:
                c.enabled = False
            def vh(v):
                c.wait(v, "act")
                c.sedge(v, v.tensor_tensor(zv(sc1[:, 0:128]), zv(zbuf[:]),
                                           av(ubuf[:]).unsqueeze(3).broadcast_to([128, 8, 4, 4]),
                                           ALU.add))
                c.tick(v.tensor_tensor(zv(sc1[:, 0:128]), zv(sc1[:, 0:128]),
                                       av(vbuf[:]).unsqueeze(2).broadcast_to([128, 8, 4, 4]),
                                       ALU.add), "dve", 1)
            c.on("dve", vh)
            def ah(s):
                c.wait(s, "dve")
                c.tick(s.activation(sc1[:, 0:128], sc1[:, 0:128], AF.Exp), "act", 1)
            c.on("act", ah)
            def pg(tt):
                c.wait(tt, lg, L0_ + 160)
                c.wait(tt, "dve")
                c.tick(tt.matmul(pxx[:, 48:49], consts["ones_r"][:], brow[0:1, 24:25],
                                 start=True, stop=True), "pe", 1)
            c.on("pe", pg)
            def vgc(v):
                c.wait(v, "pe")
                c.tick(v.tensor_copy(gcol[:, 2:3], pxx[:, 48:49]), "dve", 1)
            c.on("dve", vgc)
            def ag(s):
                c.wait(s, "dve")
                c.tick(s.activation(gcol[:, 3:4], gcol[:, 2:3], AF.Exp, scale=-1.0),
                       "act", 1)
            c.on("act", ag)
            def vg2(v):
                c.wait(v, "act")
                c.sedge(v, v.tensor_scalar_add(gcol[:, 2:3], gcol[:, 3:4], 1.0))
                c.sedge(v, v.reciprocal(gcol[:, 0:1], gcol[:, 2:3]))
                c.sedge(v, v.tensor_scalar(gcol[:, 1:2], gcol[:, 0:1], -1.0, 1.0,
                                           ALU.mult, ALU.add))
                c.wait(v, "pe")
                c.sedge(v, v.tensor_scalar_mul(sc1[:, 0:128], sc1[:, 0:128], gcol[:, 0:1]))
                c.sedge(v, v.tensor_scalar_mul(sc2[:, 0:16], gi[:], gcol[:, 1:2]))
                ii = v.tensor_tensor(zv(sc1[:, 0:128]), zv(sc1[:, 0:128]),
                                     sc2[:, 0:16].rearrange("p (i j) -> p i j", j=4)
                                     .unsqueeze(1).broadcast_to([128, 8, 4, 4]), ALU.add)
                c.tick(ii, "dve", 1)
            c.on("dve", vg2)

            hpp = e(nc.sbuf_tensor("hpp", [128, 8 * 8], F32))
            def vsm1(v):
                for idx, (off, do) in enumerate(((16, 0), (20, 4))):
                    lv = ltm[:].rearrange("p (a w) -> p a w", w=32)[:, :, off:off + 4]
                    dst = hpp[:].rearrange("p (a w) -> p a w", w=8)[:, :, do:do + 4]
                    c.sedge(v, v.reduce_max(mbuf[:, 0:8], lv, axis=AX.X))
                    mb = mbuf[:, 0:8].unsqueeze(2).broadcast_to([128, 8, 4])
                    c.tick(v.tensor_tensor(dst, lv, mb, ALU.subtract), "dve", 1)
            c.on("dve", vsm1)
            def asm1(s):
                c.wait(s, "dve")
                c.tick(s.activation(hpp[:], hpp[:], AF.Exp), "act", 1)
            c.on("act", asm1)
            def vsm2(v):
                c.wait(v, "act")
                for do in (0, 4):
                    hv = hpp[:].rearrange("p (a w) -> p a w", w=8)[:, :, do:do + 4]
                    c.sedge(v, v.reduce_sum(mbuf[:, 0:8], hv, axis=AX.X))
                    c.sedge(v, v.reciprocal(mbuf[:, 8:16], mbuf[:, 0:8]))
                    rb_ = mbuf[:, 8:16].unsqueeze(2).broadcast_to([128, 8, 4])
                    c.tick(v.tensor_tensor(hv, hv, rb_, ALU.mult), "dve", 1)
            c.on("dve", vsm2)

            if stage <= 5:
                c.enabled = False
            def emit_hrow(srccol, slot):
                Br = {k: c.cnt[k] for k in c.cnt}
                def petr(tt, srccol=srccol, Br=Br):
                    c.wait(tt, "dve")
                    for a in range(8):
                        if a >= 1:
                            tt.wait_ge(c.sem["dve"], Br["dve"] + a)
                        c.tick(tt.transpose(pxx[0:1, 128 * (a % 4):128 * (a % 4) + 128],
                                            srccol[:, a, :], consts["identf"][:]), "pe", 1)
                c.on("pe", petr)
                def vtr(v, Br=Br):
                    for a in range(8):
                        v.wait_ge(c.sem["pe"], Br["pe"] + a + 1)
                        c.tick(v.tensor_copy(hrow[0:1, a * 128:(a + 1) * 128],
                                             pxx[0:1, 128 * (a % 4):128 * (a % 4) + 128]),
                               "dve", 1)
                c.on("dve", vtr)
                def pebc(tt, Br=Br):
                    c.wait(tt, "dve")
                    for h in range(2):
                        c.tick(tt.matmul(pxx[:, h * 512:(h + 1) * 512], consts["ones_r"][:],
                                         hrow[0:1, h * 512:(h + 1) * 512],
                                         start=True, stop=True), "pe", 1)
                c.on("pe", pebc)
                def vbc(v, slot=slot, Br=Br):
                    c.wait(v, "pe")
                    c.tick(v.tensor_copy(hb[:, slot * T:(slot + 1) * T], pxx[:, 0:1024]), "dve", 1)
                c.on("dve", vbc)

            hppv = hpp[:].rearrange("p (a w) -> p a w", w=8)
            for n in range(4):
                emit_hrow(hppv[:, :, n:n + 1], 16 + n)

            def vxp(v):
                for i in range(8):
                    v.tensor_tensor(t1s[:], nsb[:, (8 * 0 + i) * T:(8 * 0 + i + 1) * T],
                                    hb[:, (16 + 0) * T:(16 + 1) * T], ALU.mult)
                    v.tensor_tensor(t2s[:], nsb[:, (8 * 1 + i) * T:(8 * 1 + i + 1) * T],
                                    hb[:, (16 + 1) * T:(16 + 2) * T], ALU.mult)
                    v.tensor_tensor(t1s[:], t1s[:], t2s[:], ALU.add)
                    v.tensor_tensor(t2s[:], nsb[:, (8 * 2 + i) * T:(8 * 2 + i + 1) * T],
                                    hb[:, (16 + 2) * T:(16 + 3) * T], ALU.mult)
                    v.tensor_tensor(t1s[:], t1s[:], t2s[:], ALU.add)
                    v.tensor_tensor(t2s[:], nsb[:, (8 * 3 + i) * T:(8 * 3 + i + 1) * T],
                                    hb[:, (16 + 3) * T:(16 + 4) * T], ALU.mult)
                    v.tensor_tensor(t2s[:], t1s[:], t2s[:], ALU.add)
                    c.tick(v.tensor_copy(xpre[:, i * T:(i + 1) * T], t2s[:]), "dve", 1)
            c.on("dve", vxp)

            sc1v = sc1[:].rearrange("p (a w) -> p a w", w=16)
            for r in range(16):
                emit_hrow(sc1v[:, :, r:r + 1], r)
            for n in range(4):
                emit_hrow(hppv[:, :, 4 + n:5 + n], 16 + n)

            if stage <= 6:
                c.enabled = False
            By = {k: c.cnt[k] for k in c.cnt}
            def pey(tt):
                c.wait(tt, "dve")
                c.wait(tt, lg, L0_ + 160)
                for o8 in range(8):
                    if o8 >= 1:
                        tt.wait_ge(c.sem["dve"], By["dve"] + o8)
                    for f8 in range(8):
                        for h in range(2):
                            ii = tt.matmul(pxx[:, (o8 % 2) * 0 + h * 512:(o8 % 2) * 0 + (h + 1) * 512],
                                           wbrs[:, f8 * 1024 + o8 * 128:f8 * 1024 + o8 * 128 + 128],
                                           xpre[:, f8 * T + h * 512:f8 * T + (h + 1) * 512],
                                           start=(f8 == 0), stop=(f8 == 7))
                    c.tick(ii, "pe", 1)
            c.on("pe", pey)
            def vey(v):
                for o8 in range(8):
                    v.wait_ge(c.sem["pe"], By["pe"] + o8 + 1)
                    c.tick(v.tensor_copy(ysb[:, o8 * T:(o8 + 1) * T], pxx[:, 0:1024]), "dve", 1)
            c.on("dve", vey)

            if stage <= 7:
                c.enabled = False
            lxf = [c.xsem("lxf0"), c.xsem("lxf1")]
            Bo = {k: c.cnt[k] for k in c.cnt}
            def loadx(sy):
                for of in range(32):
                    if of >= 2:
                        sy.wait_ge(c.sem["dve"], Bo["dve"] + of - 1)
                    c.tick(sy.dma_start(out=xf2[:, (of % 2) * T:(of % 2 + 1) * T],
                                        in_=xT[of * 128:(of + 1) * 128, :]), lxf[of % 2], 16)
            c.on("qs", loadx)
            Bqg = c.cnt["qg"]
            Bdv = c.cnt["dve"]
            def vout(v):
                for of in range(32):
                    n, i = of // 8, of % 8
                    v.wait_ge(c.sem[lxf[of % 2]], Bo[lxf[of % 2]] + 16 * (of // 2 + 1))
                    if of >= 2:
                        v.wait_ge(c.sem["qg"], Bqg + 16 * (of - 1))
                    v.tensor_tensor(t1s[:], nsb[:, (8 * 0 + i) * T:(8 * 0 + i + 1) * T],
                                    hb[:, (n * 4 + 0) * T:(n * 4 + 1) * T], ALU.mult)
                    v.tensor_tensor(t2s[:], nsb[:, (8 * 1 + i) * T:(8 * 1 + i + 1) * T],
                                    hb[:, (n * 4 + 1) * T:(n * 4 + 2) * T], ALU.mult)
                    v.tensor_tensor(t3s[:], t1s[:], t2s[:], ALU.add)
                    v.tensor_tensor(t1s[:], nsb[:, (8 * 2 + i) * T:(8 * 2 + i + 1) * T],
                                    hb[:, (n * 4 + 2) * T:(n * 4 + 3) * T], ALU.mult)
                    v.tensor_tensor(t3s[:], t3s[:], t1s[:], ALU.add)
                    v.tensor_tensor(t1s[:], nsb[:, (8 * 3 + i) * T:(8 * 3 + i + 1) * T],
                                    hb[:, (n * 4 + 3) * T:(n * 4 + 4) * T], ALU.mult)
                    v.tensor_tensor(t3s[:], t3s[:], t1s[:], ALU.add)
                    v.tensor_tensor(t1s[:], ysb[:, i * T:(i + 1) * T],
                                    hb[:, (16 + n) * T:(17 + n) * T], ALU.mult)
                    v.tensor_tensor(t3s[:], t3s[:], t1s[:], ALU.add)
                    c.tick(v.tensor_tensor(out2[:, (of % 2) * T:(of % 2 + 1) * T], t3s[:],
                                           xf2[:, (of % 2) * T:(of % 2 + 1) * T], ALU.add),
                           "dve", 1)
            c.on("dve", vout)
            def sto(g):
                for of in range(32):
                    g.wait_ge(c.sem["dve"], Bdv + of + 1)
                    c.tick(g.dma_start(out=o_x2[of * 128:(of + 1) * 128, :],
                                       in_=out2[:, (of % 2) * T:(of % 2 + 1) * T]), "qg", 16)
            c.on("pool", sto)
            def gdbg(g):
                c.tick(g.dma_start(out=o_cdbg[:, 0:128], in_=sc1[:, 0:128]), "qg", 16)
                c.tick(g.dma_start(out=o_cdbg[:, 128:192], in_=hpp[:]), "qg", 16)
                c.tick(g.dma_start(out=o_cdbg[:, 192:448], in_=ltm[:]), "qg", 16)
                c.tick(g.dma_start(out=o_cdbg[:, 448:480], in_=ubuf[:]), "qg", 16)
                c.tick(g.dma_start(out=o_cdbg[:, 480:512], in_=vbuf[:]), "qg", 16)
                c.tick(g.dma_start(out=o_cdbg2[:, 0:128], in_=zbuf[:]), "qg", 16)
                c.tick(g.dma_start(out=o_cdbg4[0:1, 49:64], in_=brow[0:1, 17:32]), "qg", 16)
                c.tick(g.dma_start(out=o_cdbg4[:, 24:26], in_=gcol[:, 0:2]), "qg", 16)
            c.on("pool", gdbg)
          if not c.enabled:
            c.enabled = True
            def dbgst(g):
                c.wait(g, "dve")
                c.wait(g, "act")
                c.wait(g, "pe")
                for f in range(NFT):
                    c.tick(g.dma_start(out=o_x2[f * 128:(f + 1) * 128, :],
                                       in_=nsb[:, f * T:(f + 1) * T]), "qg", 16)
            c.on("pool", dbgst)

        def fin(g):
            c.wait(g, "qg")
        c.on("pool", fin)
    return nc


PROJ = 18560

def build_D():
    nc = bass.Bass("TRN2", target_bir_lowering=False)
    xT = nc.dram_tensor("xT", [DM, T], F32, kind="ExternalInput").ap()
    wT = nc.dram_tensor("wT", [DM, PROJ], F32, kind="ExternalInput").ap()
    o_g = nc.dram_tensor("gateo", [8192, T], BF16, kind="ExternalOutput").ap()
    o_h = nc.dram_tensor("hso", [8192, T], BF16, kind="ExternalOutput").ap()
    o_bc = nc.dram_tensor("bco", [2048, T], BF16, kind="ExternalOutput").ap()
    o_dt = nc.dram_tensor("dto", [128, T], BF16, kind="ExternalOutput").ap()

    ones_c = nc.inline_tensor(np.ones((128, 1), dtype=BF), "c_ones_c")
    ones_r = nc.inline_tensor(np.ones((1, 128), dtype=np.float32), "c_ones_r")

    with ExitStack() as E0:
        c = BC(nc, E0.enter_context)
        consts = {}
        for nm, h, shp, dt in (("ones_c", ones_c, [128, 1], BF16),
                               ("ones_r", ones_r, [1, 128], F32)):
            consts[nm] = E0.enter_context(nc.sbuf_tensor("cs_" + nm, shp, dt))
        nsb = E0.enter_context(nc.sbuf_tensor("nsb", [128, NFT * T], BF16))
        def load_k(sy):
            for nm, h in (("ones_c", ones_c), ("ones_r", ones_r)):
                c.tick(sy.dma_start(out=consts[nm][:], in_=h.ap()), "qs", 16)
        c.on("qs", load_k)

        emit_ln(c, E0, xT, nsb, consts, rms=False, name="lnd")

        with ExitStack() as P:
            e = P.enter_context
            wsb = e(nc.sbuf_tensor("dwsb", [128, 2 * 32 * 256], BF16))
            og = e(nc.sbuf_tensor("dog", [128, 4 * T], BF16))
            pod = e(nc.psum_tensor("dpod", [128, 2 * T], F32))
            lw = [c.xsem("dw0"), c.xsem("dw1")]
            OC = 256
            NOC = PROJ // 128 // 2
            chunks = [(i * 2, 2) for i in range(72)] + [(144, 1)]

            def out_dst(gt):
                if gt < 64:
                    return o_g[gt * 128:(gt + 1) * 128, :]
                if gt < 128:
                    return o_h[(gt - 64) * 128:(gt - 63) * 128, :]
                if gt < 144:
                    return o_bc[(gt - 128) * 128:(gt - 127) * 128, :]
                return o_dt

            for ci, (gt0, ntile) in enumerate(chunks):
                base = {k: c.cnt[k] for k in c.cnt}
                buf = ci % 2
                def load(g, ci=ci, gt0=gt0, ntile=ntile, buf=buf, base=base):
                    if ci >= 2:
                        c.wait(g, "pe", base["pe"] - 1)
                    elif ci == 0:
                        c.wait(g, "dve", base["dve"])
                        c.wait(g, "pe", base["pe"])
                    src = wT[:, gt0 * 128:(gt0 + ntile) * 128] \
                        .rearrange("(g p) w -> p g w", p=128)
                    dst = wsb[:, buf * 32 * OC:buf * 32 * OC + 32 * ntile * 128] \
                        .rearrange("p (g w) -> p g w", w=ntile * 128)
                    c.tick(g.dma_start(out=dst, in_=src), lw[buf], 16)
                c.on("pool", load)

                def pemm(tt, ci=ci, gt0=gt0, ntile=ntile, buf=buf, base=base):
                    c.wait(tt, lw[buf], base[lw[buf]] + 16)
                    c.wait(tt, "dve", base["dve"])
                    for od in range(ntile):
                        for kt in range(32):
                            lhsT = wsb[:, buf * 32 * OC + kt * ntile * 128 + od * 128:
                                       buf * 32 * OC + kt * ntile * 128 + od * 128 + 128]
                            for h in range(2):
                                ii = tt.matmul(pod[:, od * T + h * 512:od * T + (h + 1) * 512],
                                               lhsT,
                                               nsb[:, kt * T + h * 512:kt * T + (h + 1) * 512],
                                               start=(kt == 0), stop=(kt == 31))
                    c.tick(ii, "pe", 1)
                c.on("pe", pemm)

                for od in range(ntile):
                    gt = gt0 + od
                    slot = gt % 4
                    def evac(v, od=od, slot=slot):
                        c.wait(v, "pe")
                        c.wait(v, "qg")
                        c.tick(v.tensor_copy(og[:, slot * T:(slot + 1) * T],
                                             pod[:, od * T:(od + 1) * T]), "dve", 1)
                    c.on("dve", evac)
                    def store(g, gt=gt, slot=slot):
                        c.wait(g, "dve")
                        c.tick(g.dma_start(out=out_dst(gt), in_=og[:, slot * T:(slot + 1) * T]),
                               "qg", 16)
                    c.on("pool", store)
        def fin(g):
            c.wait(g, "qg")
        c.on("pool", fin)
    return nc


def build_E1():
    nc = bass.Bass("TRN2", target_bir_lowering=False)
    hbch = nc.dram_tensor("hbch", [10240, 1027], BF16, kind="ExternalInput").ap()
    dto = nc.dram_tensor("dto", [128, T], BF16, kind="ExternalInput").ap()
    cw = nc.dram_tensor("cw", [128, 80 * 4], F32, kind="ExternalInput").ap()
    cb = nc.dram_tensor("cb", [128, 80], F32, kind="ExternalInput").ap()
    dtb = nc.dram_tensor("dtb", [128, 1], F32, kind="ExternalInput").ap()
    alog = nc.dram_tensor("alog", [128, 1], F32, kind="ExternalInput").ap()
    drep = nc.dram_tensor("drep", [1, 8192], F32, kind="ExternalInput").ap()
    o_yl = nc.dram_tensor("yl", [1024, 8192], BF16, kind="ExternalOutput").ap()
    o_co = nc.dram_tensor("co", [1024, T], BF16, kind="ExternalOutput").ap()
    o_sum = nc.dram_tensor("sumo", [128, 8192], BF16, kind="ExternalOutput").ap()
    o_tau = nc.dram_tensor("tauo", [128, 8], F32, kind="ExternalOutput").ap()
    o_acs = nc.dram_tensor("acsso", [128, T], F32, kind="ExternalOutput").ap()
    acfd = nc.dram_tensor("acfd", [128, 128], F32, kind="Internal").ap()

    ones_r = nc.inline_tensor(np.ones((1, 128), dtype=np.float32), "c_ones_r")
    ones_rb = nc.inline_tensor(np.ones((1, 128), dtype=BF), "c_ones_rb")
    identf = nc.inline_tensor(np.eye(128, dtype=np.float32), "c_identf")
    identb = nc.inline_tensor(np.eye(128, dtype=BF), "c_identb")
    trium = np.where(np.arange(128)[:, None] <= np.arange(128)[None, :], 0.0, -1e30)
    maskc2 = nc.inline_tensor(np.tile(trium.astype(np.float32), (1, 16)).astype(BF), "c_maskc2")

    with ExitStack() as E0:
        c = BC(nc, E0.enter_context)
        e = E0.enter_context
        consts = {}
        for nm, h, shp, dt in (("ones_r", ones_r, [1, 128], F32),
                               ("ones_rb", ones_rb, [1, 128], BF16),
                               ("identf", identf, [128, 128], F32),
                               ("identb", identb, [128, 128], BF16),
                               ("maskc2", maskc2, [128, 16 * 128], BF16)):
            consts[nm] = e(nc.sbuf_tensor("cs_" + nm, shp, dt))
        bcsb = e(nc.sbuf_tensor("bcsb", [128, 16 * T], BF16))
        srun = e(nc.sbuf_tensor("srun", [128, 8192], BF16))
        dtp = e(nc.sbuf_tensor("dtp", [128, T], F32))
        asb = e(nc.sbuf_tensor("asb", [128, T], F32))
        acs = e(nc.sbuf_tensor("acs", [128, T], F32))
        combo = e(nc.sbuf_tensor("combo", [128, T], F32))
        onesb = e(nc.sbuf_tensor("onesb", [128, 128], F32))
        cwsb = e(nc.sbuf_tensor("cwsb", [128, 80 * 4], F32))
        cbsb = e(nc.sbuf_tensor("cbsb", [128, 80], F32))
        smallc = e(nc.sbuf_tensor("smallc", [128, 8], F32))
        dwb = e(nc.sbuf_tensor("dwb", [128, 8192], BF16))
        taus = e(nc.sbuf_tensor("taus", [128, 16], F32))
        P1 = ExitStack()
        bcin = P1.enter_context(nc.sbuf_tensor("bcin", [128, 16 * 1027], BF16))
        drow = P1.enter_context(nc.sbuf_tensor("drow", [1, 8192], BF16))
        pA = e(nc.psum_tensor("e1pA", [128, 2048], F32))
        pB = e(nc.psum_tensor("e1pB", [128, 1024], F32))
        pC = e(nc.psum_tensor("e1pC", [128, 1024], F32))
        lh = c.xsem("e1lh")

        def load0(sy):
            for nm, h in (("ones_r", ones_r), ("ones_rb", ones_rb), ("identf", identf),
                          ("identb", identb), ("maskc2", maskc2)):
                c.tick(sy.dma_start(out=consts[nm][:], in_=h.ap()), "qs", 16)
            src = hbch[8192:10240, :].rearrange("(g p) w -> p g w", p=128)
            c.tick(sy.dma_start(out=bcin[:].rearrange("p (g w) -> p g w", w=1027), in_=src),
                   "qs", 16)
            c.tick(sy.dma_start(out=dtp[:].bitcast(BF16)[:, 0:T], in_=dto), "qs", 16)
        c.on("qs", load0)
        def load1(g):
            c.tick(g.dma_start(out=cwsb[:], in_=cw), "qg", 16)
            c.tick(g.dma_start(out=cbsb[:], in_=cb), "qg", 16)
            c.tick(g.dma_start(out=smallc[:, 0:1], in_=dtb), "qg", 16)
            c.tick(g.dma_start(out=smallc[:, 1:2], in_=alog), "qg", 16)
            c.tick(g.dma_start(out=drow[:], in_=drep), "qg", 16)
        c.on("pool", load1)

        B1 = {k: c.cnt[k] for k in c.cnt}
        def conv_bc(v):
            c.wait(v, "qs")
            c.wait(v, "qg")
            v.memset(onesb[:], 1.0)
            for rt in range(16):
                crt = 64 + rt
                src = bcin[:, rt * 1027:(rt + 1) * 1027]
                dst = bcsb[:, rt * T:(rt + 1) * T]
                v.tensor_scalar_mul(dst, src[:, 0:1024], cwsb[:, crt * 4:crt * 4 + 1])
                for k in range(1, 4):
                    c.sedge(v, v.scalar_tensor_tensor(dst, src[:, k:1024 + k],
                                                      cwsb[:, crt * 4 + k:crt * 4 + k + 1],
                                                      dst, ALU.mult, ALU.add))
            c.tick(v.tensor_copy(smallc[:, 2:3], smallc[:, 1:2]), "dve", 1)
        c.on("dve", conv_bc)
        def act_bc(s):
            c.wait(s, "dve")
            for rt in range(16):
                crt = 64 + rt
                c.tick(s.activation(bcsb[:, rt * T:(rt + 1) * T], bcsb[:, rt * T:(rt + 1) * T],
                                    AF.Silu, bias=cbsb[:, crt:crt + 1]), "act", 1)
            c.tick(s.activation(smallc[:, 3:4], smallc[:, 1:2], AF.Exp), "act", 1)
        c.on("act", act_bc)
        def g_co(g):
            c.wait(g, "act")
            for rt in range(8, 16):
                c.tick(g.dma_start(out=o_co[(rt - 8) * 128:(rt - 7) * 128, :],
                                   in_=bcsb[:, rt * T:(rt + 1) * T]), "qg", 16)
        c.on("pool", g_co)

        def v_dt(v):
            c.wait(v, "act")
            c.sedge(v, v.tensor_copy(asb[:], dtp[:].bitcast(BF16)[:, 0:T]))
            c.sedge(v, v.tensor_scalar_add(dtp[:], asb[:], smallc[:, 0:1]))
        c.on("dve", v_dt)
        def a_dt(s):
            c.wait(s, "dve")
            c.tick(s.activation(asb[:], dtp[:], AF.Abs), "act", 1)
            c.tick(s.activation(asb[:], asb[:], AF.Exp, scale=-1.0), "act", 1)
            c.tick(s.activation(asb[:], asb[:], AF.Ln, bias=1.0), "act", 1)
        c.on("act", a_dt)
        def v_dt2(v):
            c.wait(v, "act")
            c.sedge(v, v.tensor_scalar_max(dtp[:], dtp[:], 0.0))
            c.sedge(v, v.tensor_tensor(dtp[:], dtp[:], asb[:], ALU.add))
            c.sedge(v, v.tensor_scalar(asb[:], dtp[:], smallc[:, 3:4], -1.0,
                                       ALU.mult, ALU.mult))
        c.on("dve", v_dt2)
        D1 = {k: c.cnt[k] for k in c.cnt}
        def v_scan(v):
            for ch in range(8):
                c.tick(v.tensor_tensor_scan(acs[:, ch * 128:(ch + 1) * 128], onesb[:],
                                            asb[:, ch * 128:(ch + 1) * 128], 0.0,
                                            ALU.mult, ALU.add), "dve", 1)
            for ch in range(8):
                c.sedge(v, v.tensor_scalar(combo[:, ch * 128:(ch + 1) * 128],
                                           acs[:, ch * 128:(ch + 1) * 128], -1.0,
                                           None, ALU.mult))
                c.sedge(v, v.tensor_scalar_add(combo[:, ch * 128:(ch + 1) * 128],
                                               combo[:, ch * 128:(ch + 1) * 128],
                                               acs[:, ch * 128 + 127:ch * 128 + 128]))
                c.sedge(v, v.tensor_copy(taus[:, ch:ch + 1],
                                         acs[:, ch * 128 + 127:ch * 128 + 128]))
            c.tick(v.memset(srun[:], 0.0), "dve", 1)
        c.on("dve", v_scan)
        def a_combo(s):
            c.wait(s, "dve")
            c.tick(s.activation(combo[:], combo[:], AF.Exp), "act", 1)
            c.tick(s.activation(taus[:, 8:16], taus[:, 0:8], AF.Exp), "act", 1)
        c.on("act", a_combo)
        def g_acs(g):
            c.wait(g, "act")
            c.tick(g.dma_start(out=o_acs, in_=acs[:]), "qg", 16)
            c.tick(g.dma_start(out=o_tau, in_=taus[:, 0:8]), "qg", 16)
        c.on("pool", g_acs)

        DW = {k: c.cnt[k] for k in c.cnt}
        def pe_dw(tt):
            c.wait(tt, "qg", 16 * 5)
            c.wait(tt, "dve")
            for hh in range(16):
                if hh >= 4:
                    tt.wait_ge(c.sem["dve"], DW["dve"] + hh - 3)
                ii = tt.matmul(pA[:, (hh % 4) * 512:(hh % 4) * 512 + 512],
                               consts["ones_rb"][:],
                               drow[0:1, hh * 512:(hh + 1) * 512], start=True, stop=True)
                c.tick(ii, "pe", 1)
        c.on("pe", pe_dw)
        def v_dw(v):
            for hh in range(16):
                v.wait_ge(c.sem["pe"], DW["pe"] + hh + 1)
                c.tick(v.tensor_copy(dwb[:, hh * 512:(hh + 1) * 512],
                                     pA[:, (hh % 4) * 512:(hh % 4) * 512 + 512]), "dve", 1)
        c.on("dve", v_dw)
        P1.close()

        with ExitStack() as P2:
            e2 = P2.enter_context
            hsin = e2(nc.sbuf_tensor("hsin", [128, 64 * 131], BF16))
            hscv = e2(nc.sbuf_tensor("hscv", [128, 8192], BF16))
            hstm = e2(nc.sbuf_tensor("hstm", [128, 8192], BF16))
            xtm = e2(nc.sbuf_tensor("xtm", [128, 8192], BF16))
            gmsb = e2(nc.sbuf_tensor("gmsb", [128, 128], BF16))
            mpsb = e2(nc.sbuf_tensor("mpsb", [128, 2048], BF16))
            acst = e2(nc.sbuf_tensor("acst", [128, 128], F32))
            sdot = e2(nc.sbuf_tensor("sdot", [128, 128], BF16))
            dttm = e2(nc.sbuf_tensor("dttm", [128, 128], F32))
            comt = e2(nc.sbuf_tensor("comt", [128, 128], F32))
            btm = e2(nc.sbuf_tensor("btm", [128, 128], BF16))
            bdec = e2(nc.sbuf_tensor("bdec", [128, 2048], BF16))
            etb = e2(nc.sbuf_tensor("etb", [128, 128], BF16))
            trow = e2(nc.sbuf_tensor("trow", [1, 128], F32))
            yosb = e2(nc.sbuf_tensor("yosb", [128, 1024], F32))
            ysb1 = e2(nc.sbuf_tensor("ysb1", [128, 1024], F32))
            ysb2 = e2(nc.sbuf_tensor("ysb2", [128, 1024], BF16))
            afr = e2(nc.sbuf_tensor("afr", [1, 2048], F32))
            lhs_ = c.xsem("e1lhs")

            PBH = pB[:, 0:64].bitcast(BF16)
            PCH = pC[:, 0:64].bitcast(BF16)

            for ch in range(8):
                CB = {k: c.cnt[k] for k in c.cnt}
                def ld(sy, ch=ch, CB=CB):
                    c.wait(sy, "dve", CB["dve"])
                    if ch == 0:
                        c.wait(sy, "pe", CB["pe"])
                        c.wait(sy, "act", CB["act"])
                    src = hbch[0:8192, ch * 128:ch * 128 + 131]                         .rearrange("(g p) w -> p g w", p=128)
                    c.tick(sy.dma_start(out=hsin[:].rearrange("p (g w) -> p g w", w=131),
                                        in_=src), lhs_, 16)
                c.on("qs", ld)
                def cv(v, ch=ch, CB=CB):
                    c.wait(v, lhs_, CB[lhs_] + 16)
                    for k in range(4):
                        for rt in range(64):
                            srcw = hsin[:, rt * 131:(rt + 1) * 131]
                            dst = hscv[:, rt * 128:(rt + 1) * 128]
                            if k == 0:
                                ii = v.tensor_scalar_mul(dst, srcw[:, 0:128],
                                                         cwsb[:, rt * 4:rt * 4 + 1])
                            else:
                                ii = v.scalar_tensor_tensor(dst, srcw[:, k:128 + k],
                                                            cwsb[:, rt * 4 + k:rt * 4 + k + 1],
                                                            dst, ALU.mult, ALU.add)
                    c.tick(ii, "dve", 1)
                c.on("dve", cv)
                def cva(s, ch=ch, CB=CB):
                    c.wait(s, "dve")
                    for rt in range(64):
                        ii = s.activation(hscv[:, rt * 128:(rt + 1) * 128],
                                          hscv[:, rt * 128:(rt + 1) * 128],
                                          AF.Silu, bias=cbsb[:, rt:rt + 1])
                    c.tick(ii, "act", 1)
                c.on("act", cva)
                TB = {k: c.cnt[k] for k in c.cnt}
                def tr(tt, ch=ch, TB=TB):
                    c.wait(tt, "act")
                    c.wait(tt, "dve")
                    for j in range(64):
                        if j >= 2:
                            tt.wait_ge(c.sem["dve"], TB["dve"] + j - 1)
                        c.tick(tt.transpose(PBH if j % 2 == 0 else PCH,
                                            hscv[:, j * 128:(j + 1) * 128],
                                            consts["identb"][:]), "pe", 1)
                    for j, (srcb, w) in enumerate(((dtp, 64), (acs, 65), (combo, 66))):
                        tt.wait_ge(c.sem["dve"], TB["dve"] + 63 + j)
                        c.tick(tt.transpose(pB[:, 64:192] if (64 + j) % 2 == 0 else pC[:, 64:192],
                                            srcb[:, ch * 128:(ch + 1) * 128],
                                            consts["identf"][:]), "pe", 1)
                c.on("pe", tr)
                def trv(v, ch=ch, TB=TB):
                    for j in range(64):
                        v.wait_ge(c.sem["pe"], TB["pe"] + j + 1)
                        c.tick(v.tensor_copy(hstm[:, j * 128:(j + 1) * 128],
                                             PBH if j % 2 == 0 else PCH), "dve", 1)
                    for j, dstb in enumerate((dttm, acst, comt)):
                        v.wait_ge(c.sem["pe"], TB["pe"] + 65 + j)
                        c.tick(v.tensor_copy(dstb[:],
                                             pB[:, 64:192] if (64 + j) % 2 == 0 else pC[:, 64:192]),
                               "dve", 1)
                c.on("dve", trv)
                def vx(v, ch=ch):
                    for sl in range(16):
                        in1 = dttm[:, sl * 8:sl * 8 + 8].unsqueeze(2).broadcast_to([128, 8, 64])
                        c.tick(v.tensor_tensor(
                            xtm[:, sl * 512:(sl + 1) * 512].rearrange("p (h q) -> p h q", q=64),
                            hstm[:, sl * 512:(sl + 1) * 512].rearrange("p (h q) -> p h q", q=64),
                            in1, ALU.mult), "dve", 1)
                c.on("dve", vx)
                def va(s, ch=ch):
                    c.wait(s, "dve")
                    c.tick(s.activation(sdot[:], acst[:], AF.Exp), "act", 1)
                c.on("act", va)
                EB = {k: c.cnt[k] for k in c.cnt}
                def pet2(tt, ch=ch, EB=EB):
                    c.wait(tt, "dve")
                    c.tick(tt.transpose(pB[0:1, 256:384], taus[:, 8 + ch:9 + ch],
                                        consts["identf"][:]), "pe", 1)
                c.on("pe", pet2)
                def vet2(v, ch=ch, EB=EB):
                    c.wait(v, "pe")
                    c.tick(v.tensor_copy(trow[0:1, :], pB[0:1, 256:384]), "dve", 1)
                c.on("dve", vet2)
                def pet3(tt, ch=ch, EB=EB):
                    c.wait(tt, "dve")
                    c.tick(tt.matmul(pC[:, 256:384], consts["ones_r"][:], trow[0:1, :],
                                     start=True, stop=True), "pe", 1)
                c.on("pe", pet3)
                def vet3(v, ch=ch, EB=EB):
                    c.wait(v, "pe")
                    c.tick(v.tensor_copy(etb[:], pC[:, 256:384]), "dve", 1)
                c.on("dve", vet3)
                def gfl(g, ch=ch):
                    c.wait(g, "dve")
                    c.tick(g.dma_start(out=acfd, in_=acs[:, ch * 128:(ch + 1) * 128]), "qg", 16)
                c.on("pool", gfl)

                for g_ in range(8):
                    GB = {k: c.cnt[k] for k in c.cnt}
                    Bt = bcsb[:, g_ * T + ch * 128: g_ * T + (ch + 1) * 128]
                    Ct = bcsb[:, (8 + g_) * T + ch * 128: (8 + g_) * T + (ch + 1) * 128]
                    def gflg(g, g_=g_, GB=GB):
                        c.wait(g, "pe", GB["pe"])
                        c.wait(g, "qg")
                        c.tick(g.dma_start(
                            out=afr[0:1, :].rearrange("o (a l) -> o a l", l=128),
                            in_=acfd[:, :].rearrange("(a b) l -> b a l", b=8)[g_:g_ + 1]),
                            "qg", 16)
                    c.on("pool", gflg)
                    def pgm(tt, Bt=Bt, Ct=Ct, GB=GB):
                        c.wait(tt, "dve")
                        c.wait(tt, "qg")
                        c.tick(tt.matmul(pB[:, 512:640], Bt, Ct, start=True, stop=True),
                               "pe", 1)
                    c.on("pe", pgm)
                    def vgm(v, GB=GB):
                        c.wait(v, "pe")
                        c.tick(v.tensor_copy(gmsb[:], pB[:, 512:640]), "dve", 1)
                    c.on("dve", vgm)
                    def pd2(tt, g_=g_, GB=GB):
                        c.wait(tt, "dve")
                        c.wait(tt, "qg")
                        for q in range(4):
                            tt.matmul(pA[:, q * 512:(q + 1) * 512], consts["ones_r"][:],
                                      afr[0:1, q * 512:(q + 1) * 512],
                                      start=True, stop=False)
                            ii = tt.matmul(pA[:, q * 512:(q + 1) * 512], consts["identb"][:],
                                           consts["maskc2"][:, q * 512:q * 512 + 512],
                                           start=False, stop=True)
                        c.tick(ii, "pe", 1)
                    c.on("pe", pd2)
                    def vd2(v, g_=g_, GB=GB):
                        c.wait(v, "pe")
                        in1 = acst[:].rearrange("p (a c) -> p a c", c=8)[:, :, g_:g_ + 1]                             .broadcast_to([128, 16, 128])
                        c.tick(v.tensor_tensor(pA[:].rearrange("p (h q) -> p h q", q=128),
                                               pA[:].rearrange("p (h q) -> p h q", q=128),
                                               in1, ALU.subtract), "dve", 1)
                    c.on("dve", vd2)
                    def ad2(s, g_=g_, GB=GB):
                        c.wait(s, "dve")
                        c.tick(s.activation(mpsb[:], pA[:], AF.Exp), "act", 1)
                    c.on("act", ad2)
                    def vmp(v, g_=g_, GB=GB):
                        c.wait(v, "act")
                        in1 = gmsb[:].unsqueeze(1).broadcast_to([128, 16, 128])
                        c.tick(v.tensor_tensor(mpsb[:].rearrange("p (h q) -> p h q", q=128),
                                               mpsb[:].rearrange("p (h q) -> p h q", q=128),
                                               in1, ALU.mult), "dve", 1)
                    c.on("dve", vmp)
                    def pyy(tt, g_=g_, GB=GB):
                        c.wait(tt, "dve")
                        for h in range(16):
                            tt.matmul(pB[:, h * 64:(h + 1) * 64], Ct,
                                      srun[:, (g_ + 8 * h) * 64:(g_ + 8 * h + 1) * 64],
                                      start=(h % 8 == 0), stop=(h % 8 == 7))
                        for h in range(16):
                            ii = tt.matmul(pC[:, h * 64:(h + 1) * 64],
                                           mpsb[:, h * 128:(h + 1) * 128],
                                           xtm[:, (g_ + 8 * h) * 64:(g_ + 8 * h + 1) * 64],
                                           start=(h % 8 == 0), stop=(h % 8 == 7))
                        c.tick(ii, "pe", 1)
                    c.on("pe", pyy)
                    def vyy(v, g_=g_, ch=ch, GB=GB):
                        c.wait(v, "pe")
                        in1 = sdot[:].rearrange("p (a c) -> p a c", c=8)[:, :, g_:g_ + 1]                             .broadcast_to([128, 16, 64])
                        c.sedge(v, v.tensor_tensor(
                            yosb[:].rearrange("p (h q) -> p h q", q=64),
                            pB[:].rearrange("p (h q) -> p h q", q=64)[:, 0:16],
                            in1, ALU.mult))
                        c.sedge(v, v.tensor_tensor(ysb1[:], pC[:], yosb[:], ALU.add))
                        c.sedge(v, v.tensor_tensor(
                            yosb[:].rearrange("p (h q) -> p h q", q=64),
                            hstm[:].rearrange("p (a c) -> p a c", c=512)[:, :, g_ * 64:(g_ + 1) * 64],
                            dwb[:].rearrange("p (a c) -> p a c", c=512)[:, :, g_ * 64:(g_ + 1) * 64],
                            ALU.mult))
                        c.tick(v.tensor_tensor(ysb2[:], ysb1[:], yosb[:], ALU.add), "dve", 1)
                    c.on("dve", vyy)
                    def gy(g, g_=g_, ch=ch):
                        c.wait(g, "dve")
                        c.tick(g.dma_start(
                            out=o_yl[ch * 128:(ch + 1) * 128, :]
                            .rearrange("t (a c) -> t a c", c=512)[:, :, g_ * 64:(g_ + 1) * 64],
                            in_=ysb2[:].rearrange("t (h q) -> t h q", q=64)), "qg", 16)
                    c.on("pool", gy)
                    def pbt(tt, Bt=Bt, GB=GB):
                        c.wait(tt, "dve")
                        c.tick(tt.transpose(PBH, Bt, consts["identb"][:]), "pe", 1)
                    c.on("pe", pbt)
                    def vbt(v, g_=g_, GB=GB):
                        c.wait(v, "pe")
                        c.sedge(v, v.tensor_copy(btm[:], PBH))
                        in0 = btm[:].unsqueeze(1).broadcast_to([128, 16, 128])
                        in1 = comt[:].rearrange("p (a c) -> p a c", c=8)[:, :, g_:g_ + 1]                             .broadcast_to([128, 16, 128])
                        c.tick(v.tensor_tensor(bdec[:].rearrange("p (h q) -> p h q", q=128),
                                               in0, in1, ALU.mult), "dve", 1)
                    c.on("dve", vbt)
                    def psc(tt, g_=g_, GB=GB):
                        c.wait(tt, "dve")
                        for h in range(16):
                            ii = tt.matmul(pC[:, h * 64:(h + 1) * 64],
                                           bdec[:, h * 128:(h + 1) * 128],
                                           xtm[:, (g_ + 8 * h) * 64:(g_ + 8 * h + 1) * 64],
                                           start=(h % 8 == 0), stop=(h % 8 == 7))
                        c.tick(ii, "pe", 1)
                    c.on("pe", psc)
                    def vsu(v, g_=g_, GB=GB):
                        c.wait(v, "pe")
                        in1 = etb[:].rearrange("p (a c) -> p a c", c=8)[:, :, g_:g_ + 1] \
                            .broadcast_to([128, 16, 64])
                        sv = srun[:].rearrange("p (a c) -> p a c", c=512)[:, :,
                                                                         g_ * 64:(g_ + 1) * 64]
                        c.sedge(v, v.tensor_tensor(sv, sv, in1, ALU.mult))
                        c.tick(v.tensor_tensor(sv, sv,
                                               pC[:].rearrange("p (h q) -> p h q", q=64),
                                               ALU.add), "dve", 1)
                    c.on("dve", vsu)

            def gfin(g):
                c.wait(g, "dve")
                c.wait(g, "pe")
                c.tick(g.dma_start(out=o_sum, in_=srun[:]), "qg", 16)
            c.on("pool", gfin)
        def fin(g):
            c.wait(g, "qg")
        c.on("pool", fin)
    return nc


def build_E2():
    nc = bass.Bass("TRN2", target_bir_lowering=False)
    xT = nc.dram_tensor("xT", [DM, T], F32, kind="ExternalInput").ap()
    yl = nc.dram_tensor("yl", [1024, 8192], BF16, kind="ExternalInput").ap()
    gateo = nc.dram_tensor("gateo", [8192, T], BF16, kind="ExternalInput").ap()
    co = nc.dram_tensor("co", [1024, T], BF16, kind="ExternalInput").ap()
    acsso = nc.dram_tensor("acsso", [128, T], F32, kind="ExternalInput").ap()
    smat = nc.dram_tensor("smat", [512, 8192], BF16, kind="ExternalInput").ap()
    tmat = nc.dram_tensor("tmat", [128, 32], F32, kind="ExternalInput").ap()
    mneg = nc.dram_tensor("mneg", [128, 4], F32, kind="ExternalInput").ap()
    hsel = nc.dram_tensor("hsel", [128, 4], F32, kind="ExternalInput").ap()
    nwrep = nc.dram_tensor("nwrep", [1, 8192], F32, kind="ExternalInput").ap()
    woT = nc.dram_tensor("woT", [8192, DM], F32, kind="ExternalInput").ap()
    o_x3 = nc.dram_tensor("x3T", [DM, T], F32, kind="ExternalOutput").ap()
    ynd = nc.dram_tensor("ynd", [8192, T], BF16, kind="Internal").ap()
    trd2 = nc.dram_tensor("trd2", [4, 128], F32, kind="Internal").ap()

    ones_r = nc.inline_tensor(np.ones((1, 128), dtype=np.float32), "c_ones_r")
    ones_rb = nc.inline_tensor(np.ones((1, 128), dtype=BF), "c_ones_rb")
    identf = nc.inline_tensor(np.eye(128, dtype=np.float32), "c_identf")
    identb = nc.inline_tensor(np.eye(128, dtype=BF), "c_identb")
    MEPS = 1e-5

    with ExitStack() as E0:
        c = BC(nc, E0.enter_context)
        e = E0.enter_context
        consts = {}
        for nm, h, shp, dt in (("ones_r", ones_r, [1, 128], F32),
                               ("ones_rb", ones_rb, [1, 128], BF16),
                               ("identf", identf, [128, 128], F32),
                               ("identb", identb, [128, 128], BF16)):
            consts[nm] = e(nc.sbuf_tensor("cs_" + nm, shp, dt))
        P2 = ExitStack()
        e = P2.enter_context
        cosb = e(nc.sbuf_tensor("cosb", [128, 8 * T], BF16))
        acs = e(nc.sbuf_tensor("acs", [128, T], F32))
        scur = e(nc.sbuf_tensor("scur", [128, 8192], BF16))
        stmp = e(nc.sbuf_tensor("stmp", [128, 8192], BF16))
        tms = e(nc.sbuf_tensor("tms", [128, 24], F32))
        tall = e(nc.sbuf_tensor("tall", [128, 32], F32))
        wrow = e(nc.sbuf_tensor("wrow", [1, 512], F32))
        wb128 = e(nc.sbuf_tensor("wb128", [128, 128], BF16))
        wb4 = e(nc.sbuf_tensor("wb4", [4, 128], F32))
        nwb = e(nc.sbuf_tensor("nwb", [128, 8192], BF16))
        nwrow = e(nc.sbuf_tensor("nwrow", [1, 8192], BF16))
        taus = e(nc.sbuf_tensor("taus", [128, 16], F32))
        pA = E0.enter_context(nc.psum_tensor("e2pA", [128, 2048], F32))
        pB = E0.enter_context(nc.psum_tensor("e2pB", [128, 1024], F32))
        pC = E0.enter_context(nc.psum_tensor("e2pC", [128, 1024], F32))
        ls = c.xsem("e2ls")

        def load0(sy):
            for nm, h in (("ones_r", ones_r), ("ones_rb", ones_rb), ("identf", identf),
                          ("identb", identb)):
                c.tick(sy.dma_start(out=consts[nm][:], in_=h.ap()), "qs", 16)
            for j in range(8):
                c.tick(sy.dma_start(out=cosb[:, j * T:(j + 1) * T],
                                    in_=co[j * 128:(j + 1) * 128, :]), "qs", 16)
            c.tick(sy.dma_start(out=acs[:], in_=acsso), "qs", 16)
        c.on("qs", load0)
        def load1(g):
            c.tick(g.dma_start(out=tall[:], in_=tmat), "qg", 16)
            c.tick(g.dma_start(out=tms[:, 4:8], in_=mneg), "qg", 16)
            c.tick(g.dma_start(out=tms[:, 8:12], in_=hsel), "qg", 16)
            c.tick(g.dma_start(out=nwrow[:], in_=nwrep), "qg", 16)
        c.on("pool", load1)

        def v_ch(v):
            c.wait(v, "qg", 64)
            c.sedge(v, v.reduce_sum(tms[:, 0:4],
                                    tall[:].rearrange("p (r q) -> p r q", q=8), axis=AX.X))
            c.sedge(v, v.tensor_copy(tms[:, 12:13], tms[:, 0:1]))
            for r in range(1, 4):
                c.sedge(v, v.tensor_tensor(tms[:, 12 + r:13 + r], tms[:, 11 + r:12 + r],
                                           tms[:, r:r + 1], ALU.add))
            c.sedge(v, v.tensor_tensor(tms[:, 16:20], tms[:, 12:16], tms[:, 8:12], ALU.mult))
            c.sedge(v, v.reduce_sum(tms[:, 20:21], tms[:, 16:20], axis=AX.X))
            c.sedge(v, v.tensor_scalar(tms[:, 16:20], tms[:, 12:16], -1.0, None, ALU.mult))
            c.sedge(v, v.tensor_scalar_add(tms[:, 16:20], tms[:, 16:20], tms[:, 20:21]))
            c.tick(v.tensor_tensor(tms[:, 16:20], tms[:, 16:20], tms[:, 4:8], ALU.add),
                   "dve", 1)
        c.on("dve", v_ch)
        def a_ch(s):
            c.wait(s, "dve")
            c.tick(s.activation(tms[:, 16:20], tms[:, 16:20], AF.Exp), "act", 1)
        c.on("act", a_ch)
        def p_ch(tt):
            c.wait(tt, "act")
            c.tick(tt.transpose(pB[0:4, 0:128], tms[:, 16:20], consts["identf"][:]), "pe", 1)
        c.on("pe", p_ch)
        def v_ch2(v):
            c.wait(v, "pe")
            c.tick(v.tensor_copy(tms[0:4, 20:148] if False else wb4[:], pB[0:4, 0:128]),
                   "dve", 1)
        c.on("dve", v_ch2)
        def g_ch(g):
            c.wait(g, "dve")
            c.tick(g.dma_start(out=trd2, in_=wb4[:]), "qg", 16)
            c.wait(g, "qg")
            c.tick(g.dma_start(out=wrow[0:1, 0:512],
                               in_=trd2.rearrange("a b -> (a b)").unsqueeze(0)), "qg", 16)
        c.on("pool", g_ch)
        def v_z(v):
            c.tick(v.memset(scur[:], 0.0), "dve", 1)
        c.on("dve", v_z)
        for r in range(4):
            RB = {k: c.cnt[k] for k in c.cnt}
            def lsm(sy, r=r, RB=RB):
                c.wait(sy, "dve", RB["dve"])
                c.tick(sy.dma_start(out=stmp[:], in_=smat[r * 128:(r + 1) * 128, :]), ls, 16)
            c.on("qs", lsm)
            def pwb(tt, r=r, RB=RB):
                c.wait(tt, "qg")
                c.wait(tt, "dve")
                c.tick(tt.matmul(pB[:, 0:128], consts["ones_r"][:],
                                 wrow[0:1, r * 128:(r + 1) * 128], start=True, stop=True),
                       "pe", 1)
            c.on("pe", pwb)
            def vwb(v, r=r, RB=RB):
                c.wait(v, "pe")
                c.sedge(v, v.tensor_copy(wb128[:], pB[:, 0:128]))
                c.wait(v, ls, RB[ls] + 16)
                sv = stmp[:].rearrange("p (h q) -> p h q", q=64)
                in1 = wb128[:].unsqueeze(2).broadcast_to([128, 128, 64])
                c.sedge(v, v.tensor_tensor(sv, sv, in1, ALU.mult))
                c.tick(v.tensor_tensor(scur[:], scur[:], stmp[:], ALU.add), "dve", 1)
            c.on("dve", vwb)

        def v_taus(v):
            c.wait(v, "qs")
            for ch in range(8):
                c.sedge(v, v.tensor_copy(taus[:, ch:ch + 1],
                                         acs[:, ch * 128 + 127:ch * 128 + 128]))
            c.tick(v.tensor_copy(taus[:, 0:8], taus[:, 0:8]), "dve", 1)
        c.on("dve", v_taus)
        def a_taus(s):
            c.wait(s, "dve")
            c.tick(s.activation(taus[:, 8:16], taus[:, 0:8], AF.Exp), "act", 1)
        c.on("act", a_taus)

        if True:
            e2 = P2.enter_context
            ylt = e2(nc.sbuf_tensor("ylt", [128, 8192], BF16))
            gin = e2(nc.sbuf_tensor("gin", [128, 8192], BF16))
            gtm = e2(nc.sbuf_tensor("gtm", [128, 8192], BF16))
            acst = e2(nc.sbuf_tensor("acst2", [128, 128], F32))
            sdot = e2(nc.sbuf_tensor("sdot2", [128, 128], BF16))
            etb = e2(nc.sbuf_tensor("etb2", [128, 128], BF16))
            trow = e2(nc.sbuf_tensor("trow2", [1, 128], F32))
            ycor = e2(nc.sbuf_tensor("ycor", [128, 1024], F32))
            ysum = e2(nc.sbuf_tensor("ysum", [128, 8192], BF16))
            rst = e2(nc.sbuf_tensor("rst", [128, 8], F32))
            ysq = e2(nc.sbuf_tensor("ysq", [128, 8192], BF16))
            ystg = e2(nc.sbuf_tensor("ystg", [128, 8192], BF16))
            ly = c.xsem("e2ly")
            lg2 = c.xsem("e2lg")
            PBH = pB[:, 0:64].bitcast(BF16)
            PCH = pC[:, 0:64].bitcast(BF16)

            NB = {k: c.cnt[k] for k in c.cnt}
            def pnw(tt):
                c.wait(tt, "qg", 64)
                c.wait(tt, "dve")
                for hh in range(16):
                    if hh >= 4:
                        tt.wait_ge(c.sem["dve"], NB["dve"] + hh - 3)
                    c.tick(tt.matmul(pA[:, (hh % 4) * 512:(hh % 4) * 512 + 512],
                                     consts["ones_rb"][:],
                                     nwrow[0:1, hh * 512:(hh + 1) * 512],
                                     start=True, stop=True), "pe", 1)
            c.on("pe", pnw)
            def vnw(v):
                for hh in range(16):
                    v.wait_ge(c.sem["pe"], NB["pe"] + hh + 1)
                    c.tick(v.tensor_copy(nwb[:, hh * 512:(hh + 1) * 512],
                                         pA[:, (hh % 4) * 512:(hh % 4) * 512 + 512]), "dve", 1)
            c.on("dve", vnw)

            for ch in range(8):
                CB = {k: c.cnt[k] for k in c.cnt}
                def ldy(sy, ch=ch, CB=CB):
                    c.wait(sy, "dve", CB["dve"])
                    c.tick(sy.dma_start(out=ylt[:], in_=yl[ch * 128:(ch + 1) * 128, :]),
                           ly, 16)
                    src = gateo[:, ch * 128:(ch + 1) * 128].rearrange("(g p) w -> p g w", p=128)
                    c.tick(sy.dma_start(out=gin[:].rearrange("p (g w) -> p g w", w=128),
                                        in_=src), lg2, 16)
                c.on("qs", ldy)
                def ags(s, ch=ch, CB=CB):
                    c.wait(s, lg2, CB[lg2] + 16)
                    for j in range(64):
                        ii = s.activation(gin[:, j * 128:(j + 1) * 128],
                                          gin[:, j * 128:(j + 1) * 128], AF.Silu)
                    c.tick(ii, "act", 1)
                c.on("act", ags)
                TB = {k: c.cnt[k] for k in c.cnt}
                def tg(tt, ch=ch, TB=TB):
                    c.wait(tt, "act")
                    c.wait(tt, "dve")
                    for j in range(64):
                        if j >= 2:
                            tt.wait_ge(c.sem["dve"], TB["dve"] + j - 1)
                        c.tick(tt.transpose(PBH if j % 2 == 0 else PCH,
                                            gin[:, j * 128:(j + 1) * 128],
                                            consts["identb"][:]), "pe", 1)
                    tt.wait_ge(c.sem["dve"], TB["dve"] + 63)
                    c.tick(tt.transpose(pB[:, 64:192], acs[:, ch * 128:(ch + 1) * 128],
                                        consts["identf"][:]), "pe", 1)
                    tt.wait_ge(c.sem["dve"], TB["dve"] + 64)
                    c.tick(tt.transpose(pC[0:1, 64:192], taus[:, 8 + ch:9 + ch],
                                        consts["identf"][:]), "pe", 1)
                c.on("pe", tg)
                def tgv(v, ch=ch, TB=TB):
                    for j in range(64):
                        v.wait_ge(c.sem["pe"], TB["pe"] + j + 1)
                        c.tick(v.tensor_copy(gtm[:, j * 128:(j + 1) * 128],
                                             PBH if j % 2 == 0 else PCH), "dve", 1)
                    v.wait_ge(c.sem["pe"], TB["pe"] + 65)
                    c.sedge(v, v.tensor_copy(acst[:], pB[:, 64:192]))
                    v.wait_ge(c.sem["pe"], TB["pe"] + 66)
                    c.tick(v.tensor_copy(trow[0:1, :], pC[0:1, 64:192]), "dve", 1)
                c.on("dve", tgv)
                def sx(s, ch=ch):
                    c.wait(s, "dve")
                    c.tick(s.activation(sdot[:], acst[:], AF.Exp), "act", 1)
                c.on("act", sx)
                def pe_et(tt, ch=ch):
                    c.wait(tt, "act")
                    c.tick(tt.matmul(pC[:, 256:384], consts["ones_r"][:], trow[0:1, :],
                                     start=True, stop=True), "pe", 1)
                c.on("pe", pe_et)
                def v_et(v, ch=ch):
                    c.wait(v, "pe")
                    c.tick(v.tensor_copy(etb[:], pC[:, 256:384]), "dve", 1)
                c.on("dve", v_et)

                for g_ in range(8):
                    GB = {k: c.cnt[k] for k in c.cnt}
                    Ct = cosb[:, g_ * T + ch * 128: g_ * T + (ch + 1) * 128]
                    def pyc(tt, g_=g_, Ct=Ct, GB=GB):
                        c.wait(tt, "dve")
                        for h in range(16):
                            ii = tt.matmul(pB[:, h * 64:(h + 1) * 64], Ct,
                                           scur[:, (g_ + 8 * h) * 64:(g_ + 8 * h + 1) * 64],
                                           start=(h % 8 == 0), stop=(h % 8 == 7))
                        c.tick(ii, "pe", 1)
                    c.on("pe", pyc)
                    def vyc(v, g_=g_, ch=ch, GB=GB):
                        c.wait(v, "pe")
                        c.wait(v, ly)
                        in1 = sdot[:].rearrange("p (a c) -> p a c", c=8)[:, :, g_:g_ + 1] \
                            .broadcast_to([128, 16, 64])
                        c.sedge(v, v.tensor_tensor(ycor[:].rearrange("p (h q) -> p h q", q=64),
                                                   pB[:].rearrange("p (h q) -> p h q", q=64),
                                                   in1, ALU.mult))
                        c.sedge(v, v.tensor_tensor(
                            ycor[:].rearrange("p (h q) -> p h q", q=64),
                            ycor[:].rearrange("p (h q) -> p h q", q=64),
                            ylt[:].rearrange("p (a c) -> p a c", c=512)[:, :,
                                                                       g_ * 64:(g_ + 1) * 64],
                            ALU.add))
                        c.tick(v.tensor_tensor(
                            ysum[:].rearrange("p (a c) -> p a c", c=512)[:, :,
                                                                        g_ * 64:(g_ + 1) * 64],
                            ycor[:].rearrange("p (h q) -> p h q", q=64),
                            gtm[:].rearrange("p (a c) -> p a c", c=512)[:, :,
                                                                       g_ * 64:(g_ + 1) * 64],
                            ALU.mult), "dve", 1)
                    c.on("dve", vyc)
                def a_sq(s, ch=ch):
                    c.wait(s, "dve")
                    c.tick(s.activation(ysq[:], ysum[:], AF.Square), "act", 1)
                c.on("act", a_sq)
                def v_rms(v, ch=ch):
                    c.wait(v, "act")
                    c.sedge(v, v.reduce_sum(rst[:], ysq[:].rearrange("p (g q) -> p g q", q=1024),
                                            axis=AX.X))
                    c.sedge(v, v.tensor_scalar(rst[:], rst[:], 1.0 / 1024.0, MEPS,
                                               ALU.mult, ALU.add))
                c.on("dve", v_rms)
                def a_rms(s, ch=ch):
                    c.wait(s, "dve")
                    s.activation(rst[:], rst[:], AF.Ln)
                    c.tick(s.activation(rst[:], rst[:], AF.Exp, scale=-0.5), "act", 1)
                c.on("act", a_rms)
                def v_nrm(v, ch=ch):
                    c.wait(v, "act")
                    in1 = rst[:].unsqueeze(2).broadcast_to([128, 8, 1024])
                    c.sedge(v, v.tensor_tensor(ysum[:].rearrange("p (g q) -> p g q", q=1024),
                                               ysum[:].rearrange("p (g q) -> p g q", q=1024),
                                               in1, ALU.mult))
                    c.tick(v.tensor_tensor(ystg[:], ysum[:], nwb[:], ALU.mult), "dve", 1)
                c.on("dve", v_nrm)
                TB2 = {k: c.cnt[k] for k in c.cnt}
                def tb(tt, ch=ch, TB2=TB2):
                    c.wait(tt, "dve")
                    for j in range(64):
                        if j >= 2:
                            tt.wait_ge(c.sem["dve"], TB2["dve"] + j - 1)
                        c.tick(tt.transpose(PBH if j % 2 == 0 else PCH,
                                            ystg[:, j * 128:(j + 1) * 128],
                                            consts["identb"][:]), "pe", 1)
                c.on("pe", tb)
                def tbv(v, ch=ch, TB2=TB2):
                    for j in range(64):
                        v.wait_ge(c.sem["pe"], TB2["pe"] + j + 1)
                        c.wait(v, "qg")
                        c.tick(v.tensor_copy(ysq[:, j * 128:(j + 1) * 128],
                                             PBH if j % 2 == 0 else PCH), "dve", 1)
                c.on("dve", tbv)
                def gspill(g, ch=ch, TB2=TB2):
                    for j in range(64):
                        g.wait_ge(c.sem["dve"], TB2["dve"] + j + 1)
                        c.tick(g.dma_start(out=ynd[j * 128:(j + 1) * 128,
                                                   ch * 128:(ch + 1) * 128],
                                           in_=ysq[:, j * 128:(j + 1) * 128]), "qg", 16)
                c.on("pool", gspill)
                def v_su(v, ch=ch):
                    if ch == 7:
                        return
                    c.wait(v, "pe")
                    for g_ in range(8):
                        in1 = etb[:, g_ * 16:(g_ + 1) * 16].unsqueeze(2) \
                            .broadcast_to([128, 16, 64])
                        c.tick(v.tensor_tensor(
                            scur[:, g_ * 1024:(g_ + 1) * 1024].rearrange("p (h q) -> p h q", q=64),
                            scur[:, g_ * 1024:(g_ + 1) * 1024].rearrange("p (h q) -> p h q", q=64),
                            in1, ALU.mult), "dve", 1)
                c.on("dve", v_su)


        P2.close()

        with ExitStack() as P3:
            e3 = P3.enter_context
            ynf = e3(nc.sbuf_tensor("ynf", [128, 64 * T], BF16))
            wsb = e3(nc.sbuf_tensor("e2wsb", [128, 64 * 256], BF16))
            xf = e3(nc.sbuf_tensor("e2xf", [128, 2 * 2 * T], F32))
            og = e3(nc.sbuf_tensor("e2og", [128, 2 * 2 * T], F32))
            lw = [c.xsem("e2w0"), c.xsem("e2w1")]
            lx = [c.xsem("e2x0"), c.xsem("e2x1")]
            lyn = c.xsem("e2lyn")
            YB = {k: c.cnt[k] for k in c.cnt}
            def ldyn(sy):
                c.wait(sy, "qg")
                c.wait(sy, "dve", YB["dve"])
                c.wait(sy, "pe", YB["pe"])
                for j in range(64):
                    c.tick(sy.dma_start(out=ynf[:, j * T:(j + 1) * T],
                                        in_=ynd[j * 128:(j + 1) * 128, :]), lyn, 16)
            c.on("qs", ldyn)

            for oc in range(16):
                base = {k: c.cnt[k] for k in c.cnt}
                buf = oc % 2
                def load(g, oc=oc, buf=buf, base=base):
                    if oc >= 1:
                        c.wait(g, "pe", base["pe"])
                    src = woT[:, oc * 256:(oc + 1) * 256].rearrange("(g p) w -> p g w", p=128)
                    dst = wsb[:].rearrange("p (g w) -> p g w", w=256)
                    c.tick(g.dma_start(out=dst, in_=src), lw[buf], 16)
                c.on("pool", load)
                def loadx2(sy, oc=oc, buf=buf, base=base):
                    if oc >= 2:
                        c.wait(sy, "dve", base["dve"])
                    else:
                        c.wait(sy, "dve", YB["dve"])
                        c.wait(sy, "pe", YB["pe"])
                    for od in range(2):
                        gt = 2 * oc + od
                        c.tick(sy.dma_start(out=xf[:, (buf * 2 + od) * T:(buf * 2 + od + 1) * T],
                                            in_=xT[gt * 128:(gt + 1) * 128, :]), lx[buf], 16)
                c.on("qs", loadx2)
                def pemm(tt, oc=oc, buf=buf, base=base):
                    c.wait(tt, lw[buf], base[lw[buf]] + 16)
                    c.wait(tt, lyn, YB[lyn] + 64 * 16)
                    c.wait(tt, "dve", base["dve"])
                    for od in range(2):
                        for kt in range(64):
                            lhsT = wsb[:, kt * 256 + od * 128:kt * 256 + od * 128 + 128]
                            for h in range(2):
                                ii = tt.matmul(pA[:, od * T + h * 512:od * T + (h + 1) * 512],
                                               lhsT,
                                               ynf[:, kt * T + h * 512:kt * T + (h + 1) * 512],
                                               start=(kt == 0), stop=(kt == 63))
                    c.tick(ii, "pe", 1)
                c.on("pe", pemm)
                for od in range(2):
                    gt = 2 * oc + od
                    def evac(v, od=od, buf=buf, base=base):
                        c.wait(v, "pe")
                        c.wait(v, lx[buf], base[lx[buf]] + 32)
                        c.wait(v, "qg")
                        c.tick(v.tensor_tensor(og[:, (buf * 2 + od) * T:(buf * 2 + od + 1) * T],
                                               pA[:, od * T:(od + 1) * T],
                                               xf[:, (buf * 2 + od) * T:(buf * 2 + od + 1) * T],
                                               ALU.add), "dve", 1)
                    c.on("dve", evac)
                    def store(g, gt=gt, od=od, buf=buf):
                        c.wait(g, "dve")
                        c.tick(g.dma_start(out=o_x3[gt * 128:(gt + 1) * 128, :],
                                           in_=og[:, (buf * 2 + od) * T:(buf * 2 + od + 1) * T]),
                               "qg", 16)
                    c.on("pool", store)
        def fin(g):
            c.wait(g, "qg")
        c.on("pool", fin)
    return nc


def build_F():
    nc = bass.Bass("TRN2", target_bir_lowering=False)
    xT = nc.dram_tensor("xT", [DM, T], F32, kind="ExternalInput").ap()
    wT = nc.dram_tensor("wT", [DM, DM], F32, kind="ExternalInput").ap()
    bmat = nc.dram_tensor("bmat", [128, 32], F32, kind="ExternalInput").ap()
    wqkT = nc.dram_tensor("wqkT", [DM, 512], F32, kind="ExternalInput").ap()
    a2T = nc.dram_tensor("a2T", [DM, 128], F32, kind="ExternalInput").ap()
    b2 = nc.dram_tensor("b2", [128, DM], F32, kind="ExternalInput").ap()
    segc = nc.dram_tensor("segc", [128, 16], F32, kind="ExternalInput").ap()
    o_x4 = nc.dram_tensor("x4T", [DM, T], F32, kind="ExternalOutput").ap()
    o_fdbg = nc.dram_tensor("fdbg", [128, 2 * T + 64], F32, kind="ExternalOutput").ap()

    ones_c = nc.inline_tensor(np.ones((128, 1), dtype=BF), "c_ones_c")
    ones_r = nc.inline_tensor(np.ones((1, 128), dtype=np.float32), "c_ones_r")
    identf = nc.inline_tensor(np.eye(128, dtype=np.float32), "c_identf")
    e8 = np.zeros((128, 16), dtype=np.float32)
    for a in range(8):
        e8[:, 0] = 0.0
    inde = np.zeros((1, 8 * 128), dtype=np.float32)
    for a in range(8):
        inde[0, a * 128 + 0:a * 128 + 128] = 0.0
    ind2 = np.zeros((8, 128), dtype=np.float32)
    for a in range(8):
        ind2[a, 16 * a:16 * (a + 1)] = 1.0
    indec = nc.inline_tensor(np.ascontiguousarray(ind2.reshape(1, 1024)), "c_indec")

    ISQ32 = float(1.0 / np.sqrt(32.0))

    with ExitStack() as E0:
        c = BC(nc, E0.enter_context)
        e = E0.enter_context
        consts = {}
        for nm, h, shp, dt in (("ones_c", ones_c, [128, 1], BF16),
                               ("ones_r", ones_r, [1, 128], F32),
                               ("identf", identf, [128, 128], F32),
                               ("indec", indec, [1, 1024], F32)):
            consts[nm] = e(nc.sbuf_tensor("cs_" + nm, shp, dt))
        nsb = e(nc.sbuf_tensor("nsb", [128, NFT * T], BF16))
        segs = e(nc.sbuf_tensor("segs", [128, 16], BF16))
        bms = e(nc.sbuf_tensor("bms", [128, 32], F32))
        def load_k(sy):
            for nm, h in (("ones_c", ones_c), ("ones_r", ones_r), ("identf", identf),
                          ("indec", indec)):
                c.tick(sy.dma_start(out=consts[nm][:], in_=h.ap()), "qs", 16)
        c.on("qs", load_k)
        lsg = c.xsem("flsg")
        def load_g(g):
            c.tick(g.dma_start(out=segs[:], in_=segc), lsg, 16)
            c.tick(g.dma_start(out=bms[:], in_=bmat), lsg, 16)
        c.on("pool", load_g)

        emit_ln(c, E0, xT, nsb, consts, rms=False, name="lnf")

        with ExitStack() as P:
            e2 = P.enter_context
            qks = e2(nc.sbuf_tensor("qks", [128, 4 * T], BF16))
            hsb = e2(nc.sbuf_tensor("hsb", [128, T], BF16))
            gsb = e2(nc.sbuf_tensor("gsb", [128, T], BF16))
            b2s = e2(nc.sbuf_tensor("b2s", [128, DM], BF16))
            wsm = e2(nc.sbuf_tensor("wsm", [128, 8 * 8], F32))
            sms = e2(nc.sbuf_tensor("sms", [128, 24], F32))
            srow8 = e2(nc.sbuf_tensor("srow8", [1, 8 * T], F32))
            ssb = e2(nc.sbuf_tensor("ssb", [8, 1024], F32))
            wbc = e2(nc.sbuf_tensor("wbc", [128, T], BF16))
            srow = e2(nc.sbuf_tensor("srow", [1, T], F32))
            PQ = ExitStack()
            wqs = PQ.enter_context(nc.sbuf_tensor("wqs", [128, 32 * 512], BF16))
            a2s = PQ.enter_context(nc.sbuf_tensor("a2s", [128, 32 * 128], BF16))
            pA = e2(nc.psum_tensor("fpA", [128, 2048], F32))
            pB = e2(nc.psum_tensor("fpB", [128, 1024], F32))
            pS = e2(nc.psum_tensor("fpS", [8, 1024], F32))
            lq = c.xsem("flq")

            QB = {k: c.cnt[k] for k in c.cnt}
            def ldq(g):
                c.wait(g, "dve", QB["dve"])
                c.wait(g, "pe", QB["pe"])
                src = wqkT.rearrange("(g p) w -> p g w", p=128)
                c.tick(g.dma_start(out=wqs[:].rearrange("p (g w) -> p g w", w=512), in_=src),
                       lq, 16)
                src2 = a2T.rearrange("(g p) w -> p g w", p=128)
                c.tick(g.dma_start(out=a2s[:].rearrange("p (g w) -> p g w", w=128), in_=src2),
                       lq, 16)
                c.tick(g.dma_start(out=b2s[:], in_=b2), lq, 16)
            c.on("pool", ldq)

            for od in range(4):
                OB = {k: c.cnt[k] for k in c.cnt}
                def pq(tt, od=od, OB=OB):
                    c.wait(tt, lq, QB[lq] + 48)
                    c.wait(tt, "dve", OB["dve"])
                    for kt in range(32):
                        for hh in range(2):
                            ii = tt.matmul(pA[:, hh * 512:(hh + 1) * 512],
                                           wqs[:, kt * 512 + od * 128:kt * 512 + od * 128 + 128],
                                           nsb[:, kt * T + hh * 512:kt * T + (hh + 1) * 512],
                                           start=(kt == 0), stop=(kt == 31))
                    c.tick(ii, "pe", 1)
                c.on("pe", pq)
                def vq(v, od=od, OB=OB):
                    c.wait(v, "pe")
                    c.tick(v.tensor_copy(qks[:, od * T:(od + 1) * T], pA[:, 0:1024]), "dve", 1)
                c.on("dve", vq)
            HB = {k: c.cnt[k] for k in c.cnt}
            def ph(tt, HB=HB):
                c.wait(tt, "dve")
                for kt in range(32):
                    for hh in range(2):
                        ii = tt.matmul(pB[:, hh * 512:(hh + 1) * 512],
                                       a2s[:, kt * 128:(kt + 1) * 128],
                                       nsb[:, kt * T + hh * 512:kt * T + (hh + 1) * 512],
                                       start=(kt == 0), stop=(kt == 31))
                c.tick(ii, "pe", 1)
            c.on("pe", ph)
            def vh2(v, HB=HB):
                c.wait(v, "pe")
                c.sedge(v, v.tensor_tensor(qks[:, 0:T], qks[:, 0:T], qks[:, 2 * T:3 * T],
                                           ALU.mult))
                c.sedge(v, v.tensor_tensor(qks[:, T:2 * T], qks[:, T:2 * T],
                                           qks[:, 3 * T:4 * T], ALU.mult))
                c.tick(v.tensor_copy(hsb[:], pB[:, 0:1024]), "dve", 1)
            c.on("dve", vh2)

            PQ.close()
            SB = {k: c.cnt[k] for k in c.cnt}
            def ps2(tt, SB=SB):
                c.wait(tt, "dve")
                c.wait(tt, lsg, 32)
                for t2 in range(2):
                    for hh in range(2):
                        ii = tt.matmul(pS[:, hh * 512:(hh + 1) * 512],
                                       segs[:, t2 * 8:t2 * 8 + 8],
                                       qks[:, t2 * T + hh * 512:t2 * T + (hh + 1) * 512],
                                       start=(t2 == 0), stop=(t2 == 1))
                c.tick(ii, "pe", 1)
            c.on("pe", ps2)

            def vps(v):
                c.wait(v, "pe")
                c.tick(v.tensor_copy(ssb[:], pS[:]), "dve", 1)
            c.on("dve", vps)
            TB = {k: c.cnt[k] for k in c.cnt}
            def pt2(tt, TB=TB):
                c.wait(tt, "dve")
                for t8 in range(8):
                    if t8 >= 2:
                        tt.wait_ge(c.sem["dve"], TB["dve"] + t8 - 1)
                    c.tick(tt.transpose(pA[:, (t8 % 2) * 512:(t8 % 2) * 512 + 8],
                                        ssb[:, t8 * 128:(t8 + 1) * 128],
                                        consts["identf"][0:8, 0:8]), "pe", 1)
            c.on("pe", pt2)
            def vt2(v, TB=TB):
                for t8 in range(8):
                    v.wait_ge(c.sem["pe"], TB["pe"] + t8 + 1)
                    c.tick(v.tensor_copy(wsm[:, t8 * 8:(t8 + 1) * 8],
                                         pA[:, (t8 % 2) * 512:(t8 % 2) * 512 + 8]), "dve", 1)
            c.on("dve", vt2)
            def vs3(v):
                wv = wsm[:].rearrange("p (a w) -> p a w", w=8)
                c.sedge(v, v.tensor_scalar_mul(wsm[:], wsm[:], ISQ32))
                c.sedge(v, v.reduce_max(sms[:, 0:8], wv, axis=AX.X))
                mb = sms[:, 0:8].unsqueeze(2).broadcast_to([128, 8, 8])
                c.tick(v.tensor_tensor(wv, wv, mb, ALU.subtract), "dve", 1)
            c.on("dve", vs3)
            def as3(s):
                c.wait(s, "dve")
                c.tick(s.activation(wsm[:], wsm[:], AF.Exp), "act", 1)
            c.on("act", as3)
            def vs4(v):
                c.wait(v, "act")
                wv = wsm[:].rearrange("p (a w) -> p a w", w=8)
                c.sedge(v, v.reduce_sum(sms[:, 0:8], wv, axis=AX.X))
                c.sedge(v, v.reciprocal(sms[:, 8:16], sms[:, 0:8]))
                mb = sms[:, 8:16].unsqueeze(2).broadcast_to([128, 8, 8])
                c.tick(v.tensor_tensor(wv, wv, mb, ALU.mult), "dve", 1)
            c.on("dve", vs4)

            WB = {k: c.cnt[k] for k in c.cnt}
            def pwc(tt, WB=WB):
                c.wait(tt, "dve")
                for ei in range(8):
                    for t8 in range(8):
                        n = ei * 8 + t8
                        if n >= 2:
                            tt.wait_ge(c.sem["dve"], WB["dve"] + n - 1)
                        col = wsm[:, t8 * 8 + ei:t8 * 8 + ei + 1]
                        c.tick(tt.transpose(pA[0:1, (n % 2) * 512:(n % 2) * 512 + 128],
                                            col, consts["identf"][:]), "pe", 1)
            c.on("pe", pwc)
            def vwc(v, WB=WB):
                for ei in range(8):
                    for t8 in range(8):
                        n = ei * 8 + t8
                        v.wait_ge(c.sem["pe"], WB["pe"] + n + 1)
                        c.tick(v.tensor_copy(srow8[0:1, ei * T + t8 * 128:ei * T + (t8 + 1) * 128],
                                             pA[0:1, (n % 2) * 512:(n % 2) * 512 + 128]),
                               "dve", 1)
            c.on("dve", vwc)
            def pwc2(tt, WB=WB):
                c.wait(tt, "dve")
                for ei in range(8):
                    for hh in range(2):
                        ii = tt.matmul(pB[:, hh * 512:(hh + 1) * 512],
                                       consts["indec"][0:1, ei * 128:(ei + 1) * 128],
                                       srow8[0:1, ei * T + hh * 512:ei * T + (hh + 1) * 512],
                                       start=(ei == 0), stop=(ei == 7))
                c.tick(ii, "pe", 1)
            c.on("pe", pwc2)
            def vwc2(v, WB=WB):
                c.wait(v, "pe")
                c.sedge(v, v.tensor_copy(wbc[:], pB[:, 0:1024]))
                c.tick(v.tensor_tensor(gsb[:], hsb[:], wbc[:], ALU.mult), "dve", 1)
            c.on("dve", vwc2)

            def gdbgf(g):
                c.wait(g, "dve")
                c.tick(g.dma_start(out=o_fdbg[:, 0:64], in_=wsm[:]), "qg", 16)
                c.tick(g.dma_start(out=o_fdbg[:, 64:64 + T], in_=gsb[:]), "qg", 16)
                c.tick(g.dma_start(out=o_fdbg[:, 64 + T:64 + 2 * T], in_=hsb[:]), "qg", 16)
            c.on("pool", gdbgf)
            wsb = e2(nc.sbuf_tensor("fwsb", [128, 2 * 32 * 256], BF16))
            xf = e2(nc.sbuf_tensor("fxf", [128, 2 * 2 * T], F32))
            og = e2(nc.sbuf_tensor("fog", [128, 2 * 2 * T], F32))
            lw = [c.xsem("fw0"), c.xsem("fw1")]
            lx = [c.xsem("fx0"), c.xsem("fx1")]
            for oc in range(16):
                base = {k: c.cnt[k] for k in c.cnt}
                buf = oc % 2
                def load(g, oc=oc, buf=buf, base=base):
                    if oc >= 2:
                        c.wait(g, "pe", base["pe"] - 1)
                    elif oc == 0:
                        c.wait(g, "pe", base["pe"])
                        c.wait(g, "dve", base["dve"])
                    else:
                        c.wait(g, "pe", base["pe"] - 1)
                    src = wT[:, oc * 256:(oc + 1) * 256].rearrange("(g p) w -> p g w", p=128)
                    dst = wsb[:, buf * 32 * 256:(buf + 1) * 32 * 256] \
                        .rearrange("p (g w) -> p g w", w=256)
                    c.tick(g.dma_start(out=dst, in_=src), lw[buf], 16)
                c.on("pool", load)
                def loadx(sy, oc=oc, buf=buf, base=base):
                    if oc >= 2:
                        c.wait(sy, "dve", base["dve"])
                    else:
                        c.wait(sy, "pe", base["pe"] - oc)
                    for od in range(2):
                        gt = 2 * oc + od
                        c.tick(sy.dma_start(out=xf[:, (buf * 2 + od) * T:(buf * 2 + od + 1) * T],
                                            in_=xT[gt * 128:(gt + 1) * 128, :]), lx[buf], 16)
                c.on("qs", loadx)
                def pemm(tt, oc=oc, buf=buf, base=base):
                    c.wait(tt, lw[buf], base[lw[buf]] + 16)
                    c.wait(tt, "dve", base["dve"])
                    for od in range(2):
                        gt = 2 * oc + od
                        for kt in range(32):
                            lhsT = wsb[:, buf * 32 * 256 + kt * 256 + od * 128:
                                       buf * 32 * 256 + kt * 256 + od * 128 + 128]
                            for hh in range(2):
                                tt.matmul(pA[:, od * T + hh * 512:od * T + (hh + 1) * 512],
                                          lhsT,
                                          nsb[:, kt * T + hh * 512:kt * T + (hh + 1) * 512],
                                          start=(kt == 0), stop=False)
                        for hh in range(2):
                            ii = tt.matmul(pA[:, od * T + hh * 512:od * T + (hh + 1) * 512],
                                           b2s[:, gt * 128:(gt + 1) * 128],
                                           gsb[:, hh * 512:(hh + 1) * 512],
                                           start=False, stop=True)
                    c.tick(ii, "pe", 1)
                c.on("pe", pemm)
                for od in range(2):
                    gt = 2 * oc + od
                    def evac(v, od=od, gt=gt, buf=buf, base=base):
                        c.wait(v, "pe")
                        c.wait(v, lx[buf], base[lx[buf]] + 32)
                        c.wait(v, "qg")
                        c.sedge(v, v.tensor_scalar_add(
                            og[:, (buf * 2 + od) * T:(buf * 2 + od + 1) * T],
                            pA[:, od * T:(od + 1) * T], bms[:, gt:gt + 1]))
                        c.tick(v.tensor_tensor(og[:, (buf * 2 + od) * T:(buf * 2 + od + 1) * T],
                                               og[:, (buf * 2 + od) * T:(buf * 2 + od + 1) * T],
                                               xf[:, (buf * 2 + od) * T:(buf * 2 + od + 1) * T],
                                               ALU.add), "dve", 1)
                    c.on("dve", evac)
                    def store(g, gt=gt, od=od, buf=buf):
                        c.wait(g, "dve")
                        c.tick(g.dma_start(out=o_x4[gt * 128:(gt + 1) * 128, :],
                                           in_=og[:, (buf * 2 + od) * T:(buf * 2 + od + 1) * T]),
                               "qg", 16)
                    c.on("pool", store)
        def fin(g):
            c.wait(g, "qg")
        c.on("pool", fin)
        return nc



def kernel(**inputs):
    z = inputs
    x0 = np.asarray(z["L0_attn_hidden_states"], np.float32)
    CORES = list(range(NCORES))

    qw_cat = np.ascontiguousarray(np.concatenate(
        [z["L0_attn_qw_q"], z["L0_attn_qw_k"], z["L0_attn_qw_v"]],
        axis=1).astype(np.int8))
    sc_cat = np.ascontiguousarray(np.concatenate(
        [z["L0_attn_scales_q"], z["L0_attn_scales_k"], z["L0_attn_scales_v"]],
        axis=1).astype(np.float32))
    cos, sin = rope_tables_np()
    in_maps = []
    for c in CORES:
        b = c // 4
        pos = core_positions(c)
        in_maps.append({"xT": np.ascontiguousarray(x0[b][pos].T),
                        "qw_cat": qw_cat, "sc_cat": sc_cat,
                        "cosT": np.ascontiguousarray(cos[pos].T),
                        "sinT": np.ascontiguousarray(sin[pos].T)})
    res = run_bass_kernel_spmd(build_A(), in_maps, core_ids=CORES)
    qro = [np.asarray(res.results[c]["qro"]) for c in CORES]
    kvo = [np.asarray(res.results[c]["kv"]) for c in CORES]

    in_maps = []
    for c in CORES:
        b, j = c // 4, c % 4
        qg = np.ascontiguousarray(np.concatenate(
            [qro[4 * b + r][1024 * j:1024 * (j + 1), :] for r in range(4)], axis=1))
        kg = np.concatenate(
            [kvo[4 * b + r][256 * j:256 * (j + 1), :] for r in range(4)], axis=1)
        vg = np.concatenate(
            [kvo[4 * b + r][1024 + 256 * j:1024 + 256 * (j + 1), :] for r in range(4)],
            axis=1)
        in_maps.append({"qg": qg, "kvg": np.ascontiguousarray(
            np.concatenate([kg, vg], axis=0))})
    res = run_bass_kernel_spmd(build_B(), in_maps, core_ids=CORES)
    ctxu = [np.asarray(res.results[c]["ctxu"]) for c in CORES]

    qwo = np.ascontiguousarray(z["L0_attn_qw_o"].astype(np.int8))
    sco = np.ascontiguousarray(z["L0_attn_scales_o"].astype(np.float32))
    in_maps = []
    for c in CORES:
        b, j = c // 4, c % 4
        pos = core_positions(c)
        ctxT = np.ascontiguousarray(np.concatenate(
            [ctxu[4 * b + r][:, 1024 * j:1024 * (j + 1)] for r in range(4)], axis=0))
        in_maps.append({"xT": np.ascontiguousarray(x0[b][pos].T),
                        "ctxT": ctxT, "qwo": qwo, "sco": sco})
    res = run_bass_kernel_spmd(build_B2(), in_maps, core_ids=CORES)
    x1T = [np.asarray(res.results[c]["x1T"]) for c in CORES]

    wcat = np.ascontiguousarray(np.concatenate(
        [z["L0_ffn_W_res"].T, z["L0_ffn_W_pre"].T, z["L0_ffn_W_post"].T],
        axis=1).astype(np.float32))
    bcat = np.concatenate(
        [np.asarray(z["L0_ffn_H_res_base"]).reshape(-1),
         np.asarray(z["L0_ffn_H_pre_base"]).reshape(-1),
         np.asarray(z["L0_ffn_H_post_base"]).reshape(-1),
         np.asarray(z["L0_ffn_gate"]).reshape(-1)]).reshape(1, 25).astype(np.float32)
    wbr = np.ascontiguousarray(np.asarray(z["L0_ffn_W_branch"], np.float32))
    in_maps = [{"xT": x1T[c], "wcat": wcat, "bcat": bcat, "wbr": wbr} for c in CORES]
    res = run_bass_kernel_spmd(build_C(), in_maps, core_ids=CORES)
    x2T = [np.asarray(res.results[c]["x2T"]) for c in CORES]

    wTm = np.ascontiguousarray(np.asarray(z["L1_attn_in_proj_w"], np.float32).T)
    in_maps = [{"xT": x2T[c], "wT": wTm} for c in CORES]
    res = run_bass_kernel_spmd(build_D(), in_maps, core_ids=CORES)
    D_out = [{k: np.asarray(res.results[c][k])
              for k in ("gateo", "hso", "bco", "dto")} for c in CORES]

    cw = np.ascontiguousarray(
        np.asarray(z["L1_attn_conv_w"], np.float32).reshape(10240, 4)
        .reshape(80, 128, 4).transpose(1, 0, 2).reshape(128, 320))
    cb = np.ascontiguousarray(
        np.asarray(z["L1_attn_conv_b"], np.float32).reshape(80, 128).T)
    dtb = np.ascontiguousarray(np.asarray(z["L1_attn_dt_bias"], np.float32)
                               .reshape(128, 1))
    alog = np.ascontiguousarray(np.asarray(z["L1_attn_A_log"], np.float32)
                                .reshape(128, 1))
    drep = np.ascontiguousarray(
        np.repeat(np.asarray(z["L1_attn_D"], np.float32), 64).reshape(1, 8192))
    in_maps = []
    for c in CORES:
        b, j = c // 4, c % 4
        hbc_all = np.concatenate(
            [np.concatenate([D_out[4 * b + r]["hso"], D_out[4 * b + r]["bco"]],
                            axis=0) for r in range(4)], axis=1)
        own = hbc_all[:, 1024 * j:1024 * (j + 1)]
        halo = (np.zeros((10240, 3), dtype=BF) if j == 0
                else hbc_all[:, 1024 * j - 3:1024 * j])
        in_maps.append({"hbch": np.ascontiguousarray(
            np.concatenate([halo, own], axis=1)),
            "dto": D_out[c]["dto"], "cw": cw, "cb": cb,
            "dtb": dtb, "alog": alog, "drep": drep})
    res = run_bass_kernel_spmd(build_E1(), in_maps, core_ids=CORES)
    E1_out = [{k: np.asarray(res.results[c][k])
               for k in ("yl", "co", "sumo", "tauo", "acsso")} for c in CORES]

    nwrep = np.ascontiguousarray(
        np.asarray(z["L1_attn_norm_weight"], np.float32).reshape(1, 8192))
    woT = np.ascontiguousarray(np.asarray(z["L1_attn_out_proj_w"], np.float32).T)
    in_maps = []
    for c in CORES:
        b, j = c // 4, c % 4
        smat = np.ascontiguousarray(np.concatenate(
            [E1_out[4 * b + r]["sumo"] for r in range(4)], axis=0))
        tmat = np.ascontiguousarray(np.concatenate(
            [E1_out[4 * b + r]["tauo"] for r in range(4)], axis=1).astype(np.float32))
        mneg = np.zeros((128, 4), np.float32)
        mneg[:, j:] = -1e30
        hsel = np.zeros((128, 4), np.float32)
        if j > 0:
            hsel[:, j - 1] = 1.0
        in_maps.append({"xT": x2T[c], "yl": E1_out[c]["yl"],
                        "gateo": D_out[c]["gateo"], "co": E1_out[c]["co"],
                        "acsso": E1_out[c]["acsso"], "smat": smat, "tmat": tmat,
                        "mneg": mneg, "hsel": hsel, "nwrep": nwrep, "woT": woT})
    res = run_bass_kernel_spmd(build_E2(), in_maps, core_ids=CORES)
    x3T = [np.asarray(res.results[c]["x3T"]) for c in CORES]

    wTf = np.ascontiguousarray(np.asarray(z["L1_ffn_W"], np.float32).T)
    bmat = np.ascontiguousarray(np.asarray(z["L1_ffn_b"], np.float32)
                                .reshape(32, 128).T)
    wqkT = np.ascontiguousarray(np.concatenate(
        [np.asarray(z["L1_ffn_Wq"], np.float32).T,
         np.asarray(z["L1_ffn_Wk"], np.float32).T], axis=1))
    a2T = np.ascontiguousarray(np.asarray(z["L1_ffn_A"], np.float32)
                               .reshape(128, 4096).T)
    b2 = np.ascontiguousarray(np.asarray(z["L1_ffn_Bm"], np.float32)
                              .transpose(0, 2, 1).reshape(128, 4096))
    segc = np.zeros((128, 16), np.float32)
    for e_ in range(8):
        t2, within = divmod(e_, 4)
        segc[32 * within:32 * (within + 1), t2 * 8 + e_] = 1.0
    segc = np.ascontiguousarray(segc)
    in_maps = [{"xT": x3T[c], "wT": wTf, "bmat": bmat, "wqkT": wqkT,
                "a2T": a2T, "b2": b2, "segc": segc} for c in CORES]
    res = run_bass_kernel_spmd(build_F(), in_maps, core_ids=CORES)

    out = np.empty((2, 4096, 4096), np.float32)
    for c in CORES:
        b = c // 4
        out[b, core_positions(c), :] = np.asarray(res.results[c]["x4T"]).T
    return out
